# version 11
# baseline (speedup 1.0000x reference)
import numpy as np
import ml_dtypes
import concourse.bass as bass
import concourse.mybir as mybir
from concourse.bass_utils import run_bass_kernel_spmd
from concourse.alu_op_type import AluOpType as OP

F32 = mybir.dt.float32
BF16 = mybir.dt.bfloat16
I32 = mybir.dt.int32
AF = mybir.ActivationFunctionType
AX = mybir.AxisListType.X

NCORES = 8
TPC = 4096
T = 512
NP = TPC // T
D = 2048
ND = 16
NZ = 32
NE2 = 64
H = 5632
NH = 44
EPS = 1e-5
NWB = 7

ZIG = np.empty(NE2, np.int64)
ZIG[0::2] = np.arange(NZ)
ZIG[1::2] = np.arange(NZ, NE2)


class U:

    def __init__(self):
        import collections
        self.cnt = collections.defaultdict(int)
        self.lab = {}
        self.emit = False
        self.sems = None

    def start_emit(self, sems):
        import collections
        self.emit = True
        self.sems = sems
        self.cnt = collections.defaultdict(int)

    def inc(self, sem, n, label=None, instr=None):
        self.cnt[sem] += n
        if label is not None:
            if not self.emit:
                self.lab[label] = (sem, self.cnt[sem])
            else:
                assert self.lab[label] == (sem, self.cnt[sem]), (label, self.lab[label], sem, self.cnt[sem])
        if instr is not None:
            instr.then_inc(self.sems[sem], n)

    def wait(self, E, label):
        if E is None:
            return
        sem, val = self.lab[label]
        E.wait_ge(self.sems[sem], val)

    def alias(self, name, existing):
        if not self.emit:
            self.lab[name] = self.lab[existing]
        else:
            assert self.lab[name] == self.lab[existing]


def build_program():
    nc = bass.Bass("TRN2", target_bir_lowering=False)

    x_in = nc.dram_tensor("x", [TPC, D], F32, kind="ExternalInput")
    out_hbm = nc.dram_tensor("out", [TPC, D], F32, kind="ExternalOutput")
    winT = nc.dram_tensor("winT", [NE2, 128, D], F32, kind="ExternalInput")
    woutT = nc.dram_tensor("woutT", [4096, D], F32, kind="ExternalInput")
    wqg = nc.dram_tensor("wqg", [NH, 128, D], I32, kind="ExternalInput")
    wqu = nc.dram_tensor("wqu", [NH, 128, D], I32, kind="ExternalInput")
    wqd = nc.dram_tensor("wqd", [H, D], I32, kind="ExternalInput")
    sgc_in = nc.dram_tensor("sgc", [128, NH], F32, kind="ExternalInput")
    suc_in = nc.dram_tensor("suc", [128, NH], F32, kind="ExternalInput")
    gcc_in = nc.dram_tensor("gcc", [128, ND], F32, kind="ExternalInput")
    bcc_in = nc.dram_tensor("bcc", [128, ND], F32, kind="ExternalInput")
    sdb_in = nc.dram_tensor("sdb", [128, D], F32, kind="ExternalInput")
    dtr_in = nc.dram_tensor("dtr", [1, 4096], F32, kind="ExternalInput")
    idm_in = nc.dram_tensor("idm", [128, 128], BF16, kind="ExternalInput")

    win_s = nc.dram_tensor("win_s", [NE2, 128, D], BF16, kind="Internal")
    wout_s = nc.dram_tensor("wout_s", [4096, D], BF16, kind="Internal")
    wg_s = nc.dram_tensor("wg_s", [NH, 128, D], BF16, kind="Internal")
    wu_s = nc.dram_tensor("wu_s", [NH, 128, D], BF16, kind="Internal")
    wd_s = nc.dram_tensor("wd_s", [H, D], BF16, kind="Internal")
    dtw_s = nc.dram_tensor("dtw_s", [1, 4096], F32, kind="Internal")

    u = U()

    import contextlib
    with contextlib.ExitStack() as st:
        def sb(name, shape, dt):
            return st.enter_context(nc.sbuf_tensor(name, shape, dt))

        xres = sb("xres", [128, 4 * D], F32)
        xb2 = sb("xb2", [128, D], F32)
        xb3 = sb("xb3", [128, D], F32)
        xn4 = sb("xn4", [128, 4 * D], BF16)
        xnT = sb("xnT", [128, ND * T], BF16)
        actb = sb("actb", [128, NH * T], BF16)
        wb = [sb(f"wb{i}", [128, D], BF16) for i in range(NWB)]
        gsc = [sb(f"gsc{i}", [128, T], F32) for i in range(8)]
        qi = [sb(f"qi{i}", [128, D], I32) for i in range(2)]
        qo = [sb(f"qo{i}", [128, D], BF16) for i in range(2)]
        sdbt = sb("sdbt", [128, D], F32)
        sgct = sb("sgct", [128, NH], F32)
        suct = sb("suct", [128, NH], F32)
        gcct = sb("gcct", [128, ND], F32)
        bcct = sb("bcct", [128, ND], F32)
        dtwt = sb("dtwt", [128, NZ], F32)
        idmt = sb("idmt", [128, 128], BF16)
        bnb = sb("bnb", [128, 24], F32)
        m128 = sb("m128", [128, 1], F32)
        mvb = sb("mvb", [128, 16], F32)
        coef = sb("coef", [128, 24], F32)
        ntmp = sb("ntmp", [128, 24], F32)

        pb = [st.enter_context(nc.psum_tensor(f"pb{i}", [128, T], F32)) for i in range(8)]

        sem_names = (["t", "v", "s", "g", "cst", "dt1", "dt2", "dt3", "od0", "od1", "od2", "od3"]
                     + [f"wb{i}" for i in range(NWB)]
                     + [f"qi{i}" for i in range(4)]
                     + [f"qs{i}" for i in range(4)]
                     + [f"x{i}" for i in range(8)])
        sems = {n: st.enter_context(nc.semaphore("s_" + n)) for n in sem_names}

        qi_f32 = [qi[0][:, :].bitcast(F32), qi[1][:, :].bitcast(F32)]
        ring_i32 = [qi[0][:, :], qi[1][:, :], xb2[:, :].bitcast(I32), xb3[:, :].bitcast(I32)]
        ring_f32 = [qi_f32[0], qi_f32[1], xb2[:, :], xb3[:, :]]

        def xslot(p, tc):
            if p % 2 == 0:
                return xres[:, tc * D:(tc + 1) * D]
            return [xb2[:, :], xb3[:, :], qi_f32[0], qi_f32[1]][tc]

        def xslot_j(p, tc, j):
            s = xslot(p, tc)
            if p % 2 == 0:
                return xres[:, tc * D + j * T: tc * D + (j + 1) * T]
            return [xb2, xb3, None, None][tc][:, j * T:(j + 1) * T] if tc < 2 else qi_f32[tc - 2][:, j * T:(j + 1) * T]

        dt_row = actb[0:1, 0:8192].bitcast(F32)
        dt_exp = actb[0:1, 8192:16384].bitcast(F32)

        def slab_list(p):
            out = []
            for e in range(NZ):
                out.append(("win", 2 * e))
                out.append(("win", 2 * e + 1))
            for P2 in range(2):
                for e in range(NZ):
                    out.append(("wout", e, P2))
            for h in range(NH):
                out.append(("wg", h))
                out.append(("wu", h))
            for P2 in range(2):
                for h in range(NH):
                    out.append(("wd", h, P2))
            return out

        NSLAB = len(slab_list(0))

        def k2g_m2(p, P2):
            return p * NSLAB + 2 * NZ + (P2 + 1) * NZ - 1

        ITEMS = []
        for e in range(NZ):
            ITEMS.append(("win", 2 * e))
            ITEMS.append(("win", 2 * e + 1))
            ITEMS.append(("wout", e))
        for h in range(NH):
            ITEMS.append(("wg", h))
            ITEMS.append(("wu", h))
            ITEMS.append(("wd", h))
        NIT = len(ITEMS)

        def item_of_slab(kind, idx):
            if kind == "win":
                return 3 * (idx // 2) + (idx % 2)
            if kind == "wout":
                return 3 * idx + 2
            base = 3 * NZ
            off = {"wg": 0, "wu": 1, "wd": 2}[kind]
            return base + 3 * idx + off

        def item_src(i_):
            kind, idx = ITEMS[i_]
            if kind == "win":
                return winT[idx]
            if kind == "wout":
                return woutT[idx * 128:(idx + 1) * 128, :]
            if kind == "wg":
                return wqg[idx]
            if kind == "wu":
                return wqu[idx]
            return wqd[idx * 128:(idx + 1) * 128, :]

        def p0_direct(kind):
            return kind in ("win", "wg", "wu")

        def slab_of_item(i_):
            kind, idx = ITEMS[i_]
            if kind == "win":
                return idx
            if kind == "wout":
                return 2 * NZ + idx
            if kind == "wg":
                return 4 * NZ + 2 * idx
            if kind == "wu":
                return 4 * NZ + 2 * idx + 1
            return 4 * NZ + 2 * NH + idx

        QOSLOT = {}
        _qct = 0
        for _i, (_k, _x) in enumerate(ITEMS):
            if not p0_direct(_k):
                QOSLOT[_i] = _qct % 2
                _qct += 1

        QOPREV = {}
        _last = {0: None, 1: None}
        for _i in sorted(QOSLOT):
            QOPREV[_i] = _last[QOSLOT[_i]]
            _last[QOSLOT[_i]] = _i

        def item_dst(i_):
            kind, idx = ITEMS[i_]
            if kind == "win":
                return win_s[idx]
            if kind == "wout":
                return wout_s[idx * 128:(idx + 1) * 128, :]
            if kind == "wg":
                return wg_s[idx]
            if kind == "wu":
                return wu_s[idx]
            return wd_s[idx * 128:(idx + 1) * 128, :]

        def newton_rsqrt(E, op, vp_ap, w, iters):
            A = ntmp[:, 0:w]
            B = ntmp[:, 4:4 + w]
            Dv = ntmp[:, 8:8 + w]
            Ev = ntmp[:, 12:12 + w]
            op(lambda: E.tensor_single_scalar(Dv, vp_ap, 0.4, OP.max))
            op(lambda: E.reciprocal(A, Dv))
            y, yn = A, B
            for _ in range(iters):
                op(lambda y=y: E.tensor_tensor(Dv, y, y, OP.mult))
                op(lambda: E.tensor_tensor(Ev, Dv, vp_ap, OP.mult))
                op(lambda: E.tensor_scalar(Ev, Ev, -0.5, 1.5, OP.mult, OP.add))
                op(lambda y=y, yn=yn: E.tensor_tensor(yn, y, Ev, OP.mult))
                y, yn = yn, y
            return y

        def wq_conv_sem(i_):
            return "s" if NH <= i_ < 2 * NH else "v"

        def gen_gpsimd(E):
            def dma(sem, out, in_, label=None, **kw):
                i = E.dma_start(out, in_, **kw) if E else None
                u.inc(sem, 16, label, i)

            for tc in range(4):
                dma(f"x{tc}", xslot(0, tc), x_in[tc * 128:(tc + 1) * 128, :], label=f"xin:0:{tc}")
            dma("cst", sgct[:, :], sgc_in[:, :])
            dma("cst", suct[:, :], suc_in[:, :])
            dma("cst", gcct[:, :], gcc_in[:, :])
            dma("cst", bcct[:, :], bcc_in[:, :])
            dma("cst", sdbt[:, :], sdb_in[:, :])
            dma("cst", idmt[:, :], idm_in[:, :], label="cst")
            dma("dt1", dt_row, dtr_in[:, :], label="dtin")
            def qld(i_):
                kind = ITEMS[i_][0]
                dst = ring_f32[i_ % 4] if kind in ("win", "wout") else ring_i32[i_ % 4]
                ii = E.dma_start(dst, item_src(i_)) if E else None
                u.inc(f"qi{i_ % 4}", 16, f"qld:{i_}", ii)

            for i_ in range(4):
                qld(i_)
            u.wait(E, "dtwrow")
            dma("dt2", dtw_s[:, :], dt_row, label="dtwst")
            u.wait(E, "dtwst")
            in_ap = dtw_s[:, :].rearrange("a (j p) -> p (a j)", p=128) if E else None
            i2 = E.dma_start(dtwt[:, 0:NZ], in_ap, allow_slow_non_contiguous=True) if E else None
            u.inc("dt3", 16, "dtwcols", i2)
            for i_ in range(NIT):
                if i_ >= 1:
                    u.wait(E, f"wqst:{i_ - 1}")
                    if E:
                        E.sem_inc(sems["g"], 1)
                    u.inc("g", 1, f"cf:{i_ - 1}", None)
                u.wait(E, f"wqts:{i_}")
                if p0_direct(ITEMS[i_][0]):
                    st_src = wb[slab_of_item(i_) % NWB][:, :]
                else:
                    st_src = qo[QOSLOT[i_]][:, :]
                ii = E.dma_start(item_dst(i_), st_src) if E else None
                u.inc(f"qs{i_ % 4}", 16, f"wqst:{i_}", ii)
                if i_ + 4 < NIT:
                    qld(i_ + 4)
            u.wait(E, f"wqst:{NIT - 1}")
            if E:
                E.sem_inc(sems["g"], 1)
            u.inc("g", 1, f"cf:{NIT - 1}", None)
            u.wait(E, f"wqts:{NIT - 2}")
            dma("x4", xslot(1, 0), x_in[T:T + 128, :], label="xin:1:0")
            u.wait(E, f"wqts:{NIT - 1}")
            dma("x5", xslot(1, 1), x_in[T + 128:T + 256, :], label="xin:1:1")
            u.wait(E, f"wqts:{NIT - 4}")
            dma("x6", xslot(1, 2), x_in[T + 256:T + 384, :], label="xin:1:2")
            u.wait(E, f"wqts:{NIT - 3}")
            dma("x7", xslot(1, 3), x_in[T + 384:T + 512, :], label="xin:1:3")

            for p in range(NP):
                for tc in range(4):
                    r0 = p * T + tc * 128
                    for j in range(4):
                        u.wait(E, f"fin:{p}:{tc}" if j == 3 else f"finj:{p}:{tc}:{j}")
                        ii = E.dma_start(out_hbm[r0:r0 + 128, j * T:(j + 1) * T],
                                         gsc[2 * j + 1][:, :]) if E else None
                        u.inc(f"od{j}", 16, f"oD:{p}:{tc}:{j}", ii)
                    if p + 2 < NP:
                        r1 = (p + 2) * T + tc * 128
                        ii = E.dma_start(xslot(p + 2, tc), x_in[r1:r1 + 128, :]) if E else None
                        u.inc(f"x{4 * (p % 2) + tc}", 16, f"xin:{p + 2}:{tc}", ii)
            if E:
                for jj in range(4):
                    E.wait_ge(sems[f"od{jj}"], u.cnt[f"od{jj}"])

        def gen_sync(E):
            g = 0
            NS0 = len(slab_list(0))
            for p in range(NP):
                for item in slab_list(p):
                    kind = item[0]
                    if p == 0 and p0_direct(kind):
                        g += 1
                        continue
                    if g >= NWB:
                        u.wait(E, f"su:{g - NWB}")
                        gprev = g - NWB
                        if gprev < NS0:
                            pk = slab_list(0)[gprev][0]
                            if p0_direct(pk):
                                u.wait(E, f"wqst:{item_of_slab(pk, slab_list(0)[gprev][1])}")
                    if p <= 1:
                        u.wait(E, f"cf:{item_of_slab(kind, item[1])}")
                    ii = None
                    if E:
                        slot = wb[g % NWB]
                        if kind == "win":
                            src = win_s[item[1]]
                        elif kind == "wout":
                            src = wout_s[item[1] * 128:(item[1] + 1) * 128, :]
                        elif kind == "wg":
                            src = wg_s[item[1]]
                        elif kind == "wu":
                            src = wu_s[item[1]]
                        else:
                            src = wd_s[item[1] * 128:(item[1] + 1) * 128, :]
                        ii = E.dma_start(slot[:, :], src)
                    u.inc(f"wb{g % NWB}", 16, f"sl:{g}", ii)
                    g += 1

        def gen_tensor(E):
            gslab = [0]

            def mm(out_ap, lhsT, rhs, start, stop, inc_label=None):
                i = E.matmul(out_ap, lhsT, rhs, start=start, stop=stop) if E else None
                if inc_label is not None:
                    u.inc("t", 1, inc_label, i)

            for p in range(NP):
                if p > 0:
                    u.wait(E, f"finj:{p - 1}:2:1")
                else:
                    u.wait(E, "cst")
                for k in range(ND):
                    if k >= 2:
                        u.wait(E, f"T1ev:{p}:{k - 2}")
                    for tc in range(4):
                        if k == 0:
                            u.wait(E, f"xn1:{p}:{tc}")
                        mm(pb[4 + k % 2][:, tc * 128:(tc + 1) * 128],
                           xn4[:, tc * D + k * 128: tc * D + (k + 1) * 128] if E else None,
                           idmt[:, :] if E else None, True, True,
                           inc_label=(f"T1mm:{p}:{k}" if tc == 3 else None))
                u.wait(E, f"T1ev:{p}:{ND - 1}")
                for e in range(NZ):
                    if p > 0 and e == 0:
                        u.wait(E, f"finj:{p - 1}:3:1")
                    if p > 0 and e == 1:
                        u.wait(E, f"fin:{p - 1}:3")
                    if e >= 2:
                        u.wait(E, f"gat:{p}:{e - 2}")
                    for half in range(2):
                        sg = gslab[0]
                        u.wait(E, f"sl:{sg}")
                        pz = pb[2 * (e % 2) + half]
                        for k in range(ND):
                            mm(pz[:, :], wb[sg % NWB][:, k * 128:(k + 1) * 128] if E else None,
                               xnT[:, k * T:(k + 1) * T] if E else None,
                               k == 0, k == ND - 1,
                               inc_label=(f"su:{sg}" if k == ND - 1 else None))
                        gslab[0] += 1
                for P2 in range(2):
                    for e in range(NZ):
                        sg = gslab[0]
                        u.wait(E, f"sl:{sg}")
                        if P2 == 0:
                            u.wait(E, f"gat:{p}:{e}")
                        for tci in range(2):
                            tc = 2 * P2 + tci
                            if e == 0:
                                if P2 == 0 and tci == 1:
                                    u.wait(E, f"gat:{p}:{NZ - 1}")
                            for j in range(4):
                                if e == 0 and P2 == 1:
                                    u.wait(E, f"ev:{p}:{tci}:{j}")
                                mm(pb[(4 if tci == 0 else 0) + j][:, 0:T],
                                   actb[:, e * T + tc * 128: e * T + (tc + 1) * 128] if E else None,
                                   wb[sg % NWB][:, j * T:(j + 1) * T] if E else None,
                                   e == 0, e == NZ - 1,
                                   inc_label=(f"su:{sg}" if (tci == 1 and j == 3) else None))
                        gslab[0] += 1
                u.wait(E, f"ev:{p}:2:1")
                for k in range(ND):
                    if k >= 2:
                        u.wait(E, f"T2evA:{p}:{k - 2}")
                    for tc in range(2):
                        if k == 0:
                            u.wait(E, f"xnf:{p}:{tc}")
                        mm(pb[4 + k % 2][:, tc * 128:(tc + 1) * 128],
                           xn4[:, tc * D + k * 128: tc * D + (k + 1) * 128] if E else None,
                           idmt[:, :] if E else None, True, True,
                           inc_label=(f"T2Amm:{p}:{k}" if tc == 1 else None))
                u.wait(E, f"T2evA:{p}:{ND - 1}")
                for k in range(ND):
                    if k >= 2:
                        u.wait(E, f"T2evB:{p}:{k - 2}")
                    for tc in range(2, 4):
                        if k == 0:
                            u.wait(E, f"xnf:{p}:{tc}")
                        mm(pb[4 + k % 2][:, tc * 128:(tc + 1) * 128],
                           xn4[:, tc * D + k * 128: tc * D + (k + 1) * 128] if E else None,
                           idmt[:, :] if E else None, True, True,
                           inc_label=(f"T2Bmm:{p}:{k}" if tc == 3 else None))
                u.wait(E, f"T2evB:{p}:{ND - 1}")
                for h in range(NH):
                    if h == 0:
                        u.wait(E, f"ev:{p}:3:1")
                    if h == 1:
                        u.wait(E, f"ev:{p}:3:3")
                    if h >= 2:
                        u.wait(E, f"actf:{p}:{h - 2}")
                    for half in range(2):
                        sg = gslab[0]
                        u.wait(E, f"sl:{sg}")
                        pz = pb[2 * (h % 2) + half]
                        for k in range(ND):
                            mm(pz[:, :], wb[sg % NWB][:, k * 128:(k + 1) * 128] if E else None,
                               xnT[:, k * T:(k + 1) * T] if E else None,
                               k == 0, k == ND - 1,
                               inc_label=(f"su:{sg}" if k == ND - 1 else None))
                        gslab[0] += 1
                for P2 in range(2):
                    if P2 == 0:
                        u.wait(E, f"actf:{p}:{NH - 1}")
                    for h in range(NH):
                        sg = gslab[0]
                        u.wait(E, f"sl:{sg}")
                        for tci in range(2):
                            tc = 2 * P2 + tci
                            for j in range(4):
                                if h == 0 and P2 == 1:
                                    u.wait(E, f"fin:{p}:{tci}" if j == 3 else f"finj:{p}:{tci}:{j}")
                                mm(pb[(4 if tci == 0 else 0) + j][:, 0:T],
                                   actb[:, h * T + tc * 128: h * T + (tc + 1) * 128] if E else None,
                                   wb[sg % NWB][:, j * T:(j + 1) * T] if E else None,
                                   h == 0, h == NH - 1,
                                   inc_label=(f"su:{sg}" if (tci == 1 and j == 3) else None))
                        gslab[0] += 1

        def gen_vector2(E):
            def op(fn, label=None):
                if E:
                    E.wait_ge(sems["v"], u.cnt["v"])
                i = fn() if E else None
                u.inc("v", 1, label, i)

            def conv_op(i_):
                u.wait(E, f"qld:{i_}")
                kind = ITEMS[i_][0]
                if p0_direct(kind):
                    g_ = slab_of_item(i_)
                    out_ap = wb[g_ % NWB][:, :]
                    if g_ >= NWB:
                        u.wait(E, f"su:{g_ - NWB}")
                        pk, px = slab_list(0)[g_ - NWB][0], slab_list(0)[g_ - NWB][1]
                        if p0_direct(pk):
                            u.wait(E, f"wqst:{item_of_slab(pk, px)}")
                else:
                    out_ap = qo[QOSLOT[i_]][:, :]
                    if QOPREV[i_] is not None:
                        u.wait(E, f"wqst:{QOPREV[i_]}")
                if kind in ("win", "wout"):
                    op(lambda: E.tensor_copy(out_ap, ring_f32[i_ % 4]), f"wqts:{i_}")
                elif kind in ("wg", "wu"):
                    op(lambda: E.tensor_single_scalar(out_ap, ring_i32[i_ % 4], -128.0, OP.add), f"wqts:{i_}")
                else:
                    op(lambda: E.scalar_tensor_tensor(out_ap, ring_i32[i_ % 4], -128.0,
                                                      sdbt[:, :], OP.add, OP.mult), f"wqts:{i_}")
                if p0_direct(kind):
                    u.alias(f"sl:{slab_of_item(i_)}", f"wqts:{i_}")

            u.wait(E, "dtin")
            op(lambda: E.tensor_reduce(ntmp[0:1, 0:1], dt_row, AX, OP.min), "dtmin")
            u.wait(E, "dtexp")
            op(lambda: E.tensor_reduce(ntmp[0:1, 1:2], dt_exp, AX, OP.add))
            op(lambda: E.reciprocal(ntmp[0:1, 2:3], ntmp[0:1, 1:2]))
            op(lambda: E.tensor_single_scalar(dt_row, dt_exp, ntmp[0:1, 2:3], OP.mult), "dtwrow")

            def ln_stats(p, tc, base):
                for j in range(4):
                    op(lambda p=p, tc=tc, j=j: E.bn_stats(bnb[:, 6 * j: 6 * (j + 1)], xslot_j(p, tc, j)))
                op(lambda tc=tc, base=base: E.bn_aggr(mvb[:, base + 2 * tc: base + 2 * tc + 2], bnb[:, 0:24]))

            def ln1_block(p):
                for tc in range(4):
                    u.wait(E, f"xin:{p}:{tc}")
                    ln_stats(p, tc, 0)
                w = 4
                mu = mvb[:, 0:8:2]
                v1 = mvb[:, 1:8:2]
                C = ntmp[:, 16:16 + w]
                Dv = ntmp[:, 8:8 + w]
                op(lambda: E.tensor_single_scalar(C, v1, EPS, OP.add))
                y1 = newton_rsqrt(E, op, C, w, 7)
                op(lambda y1=y1: E.tensor_single_scalar(coef[:, 8:12], y1, 1.0, OP.add))
                op(lambda y1=y1: E.tensor_tensor(Dv, mu, y1, OP.mult))
                op(lambda: E.tensor_single_scalar(coef[:, 12:16], Dv, -1.0, OP.mult))
                op(lambda: E.reciprocal(Dv, C))
                op(lambda: E.tensor_tensor(Dv, v1, Dv, OP.mult))
                op(lambda y1=y1: E.tensor_copy(coef[:, 0:4], y1))
                op(lambda: E.tensor_single_scalar(C, Dv, EPS, OP.add))
                y2 = newton_rsqrt(E, op, C, w, 4)
                op(lambda y2=y2: E.tensor_tensor(coef[:, 0:4], coef[:, 0:4], y2, OP.mult))
                op(lambda: E.tensor_tensor(Dv, mu, coef[:, 0:4], OP.mult))
                op(lambda: E.tensor_single_scalar(coef[:, 4:8], Dv, -1.0, OP.mult), f"st1:{p}")

            for p in range(NP):
                if p == 0:
                    ln1_block(0)
                    u.wait(E, "cst")
                    for i_ in range(6):
                        conv_op(i_)
                for e in range(NZ):
                    sgg = p * NSLAB + 2 * e + 1
                    u.wait(E, f"su:{sgg}")
                    if p == 0 and e == 0:
                        u.wait(E, "dtwcols")
                    b = 4 * (e % 2)
                    pz, pg = pb[2 * (e % 2)], pb[2 * (e % 2) + 1]
                    op(lambda pz=pz, b=b: E.tensor_scalar(gsc[b][:, :], pz[:, :], 15.0, -15.0, OP.min, OP.max), f"zc:{p}:{e}")
                    op(lambda pg=pg, b=b: E.tensor_scalar(gsc[b + 2][:, :], pg[:, :], 15.0, -15.0, OP.min, OP.max), f"gc:{p}:{e}")
                    u.wait(E, f"sg2:{p}:{e}")
                    op(lambda b=b: E.tensor_tensor(gsc[b][:, :], gsc[b + 2][:, :], gsc[b + 3][:, :], OP.mult))
                    op(lambda pz=pz, b=b: E.tensor_tensor(gsc[b + 2][:, :], pz[:, :], gsc[b + 1][:, :], OP.mult))
                    op(lambda b=b, e=e: E.scalar_tensor_tensor(actb[:, e * T:(e + 1) * T], gsc[b + 2][:, :],
                                                               dtwt[:, e:e + 1], gsc[b][:, :], OP.mult, OP.mult), f"gat:{p}:{e}")
                    if p == 0 and e + 2 < NZ:
                        for i_ in range(3 * (e + 2), 3 * (e + 2) + 3):
                            conv_op(i_)
                for P2 in range(2):
                    last_slab = p * NSLAB + 2 * NZ + (P2 + 1) * NZ - 1
                    u.wait(E, f"su:{last_slab}")
                    for tci in range(2):
                        tc = 2 * P2 + tci
                        for j in range(4):
                            u.wait(E, f"ev:{p}:{tc}:{j}")
                            op(lambda p=p, tc=tc, j=j: E.scalar_tensor_tensor(
                                xslot_j(p, tc, j), xslot_j(p, tc, j), coef[:, 8 + tc:9 + tc],
                                gsc[4 * (j % 2)][:, :], OP.mult, OP.add),
                                f"x2:{p}:{tc}" if j == 3 else f"x2j:{p}:{tc}:{j}")
                        ln_stats(p, tc, 8)
                    w = 2
                    mu = mvb[:, 8 + 4 * P2: 8 + 4 * P2 + 3:2]
                    v1 = mvb[:, 9 + 4 * P2: 9 + 4 * P2 + 3:2]
                    C = ntmp[:, 16:16 + w]
                    Dv = ntmp[:, 8:8 + w]
                    op(lambda v1=v1, C=C: E.tensor_single_scalar(C, v1, EPS, OP.add))
                    yf = newton_rsqrt(E, op, C, w, 7)
                    op(lambda yf=yf, P2=P2: E.tensor_single_scalar(coef[:, 16 + 2 * P2:18 + 2 * P2], yf, 1.0, OP.mult))
                    op(lambda mu=mu, yf=yf, Dv=Dv: E.tensor_tensor(Dv, mu, yf, OP.mult))
                    op(lambda Dv=Dv, P2=P2: E.tensor_single_scalar(coef[:, 20 + 2 * P2:22 + 2 * P2], Dv, -1.0, OP.mult), f"st2:{p}:{P2}")
                    if p == 0 and P2 == 1:
                        for k in range(9):
                            conv_op(3 * NZ + k)
                for h in range(NH):
                    sgu = p * NSLAB + 4 * NZ + 2 * h + 1
                    u.wait(E, f"su:{sgu}")
                    u.wait(E, f"sil:{p}:{h}")
                    op(lambda h=h: E.scalar_tensor_tensor(actb[:, h * T:(h + 1) * T],
                                                          pb[2 * (h % 2) + 1][:, :], suct[:, h:h + 1],
                                                          gsc[4 * (h % 2)][:, :], OP.mult, OP.mult), f"actf:{p}:{h}")
                    if p == 0 and h <= 40:
                        for i_ in range(3 * NZ + 9 + 3 * h, 3 * NZ + 9 + 3 * h + 3):
                            conv_op(i_)
                if p + 1 < NP:
                    ln1_block(p + 1)
                for P2 in range(2):
                    last_slab = p * NSLAB + 4 * NZ + 2 * NH + (P2 + 1) * NH - 1
                    u.wait(E, f"su:{last_slab}")
                    for tci in range(2):
                        tc = 2 * P2 + tci
                        for j in range(4):
                            if tc > 0:
                                u.wait(E, f"oD:{p}:{tc - 1}:{j}")
                            elif p > 0:
                                u.wait(E, f"oD:{p - 1}:3:{j}")
                            op(lambda p=p, tc=tc, tci=tci, j=j: E.tensor_tensor(
                                gsc[2 * j + 1][:, :], xslot_j(p, tc, j),
                                pb[(4 if tci == 0 else 0) + j][:, :], OP.add),
                                f"fin:{p}:{tc}" if j == 3 else f"finj:{p}:{tc}:{j}")

        def gen_scalar(E):
            def op(fn, label=None):
                if E:
                    E.wait_ge(sems["s"], u.cnt["s"])
                i = fn() if E else None
                u.inc("s", 1, label, i)

            u.wait(E, "dtmin")
            op(lambda: E.activation(dt_exp, dt_row, AF.Exp, scale=-1.0, bias=ntmp[0:1, 0:1]), "dtexp")
            for p in range(NP):
                if p == 0:
                    u.wait(E, "st1:0")
                    for tc in range(4):
                        op(lambda tc=tc: E.activation(xn4[:, tc * D:(tc + 1) * D], xslot(0, tc), AF.Identity,
                                                      scale=coef[:, 0 + tc:1 + tc], bias=coef[:, 4 + tc:5 + tc]),
                           f"xn1:0:{tc}")
                for k in range(ND):
                    u.wait(E, f"T1mm:{p}:{k}")
                    if p == 0 and k == 0:
                        u.wait(E, "cst")
                    op(lambda k=k: E.activation(xnT[:, k * T:(k + 1) * T], pb[4 + k % 2][:, :], AF.Identity,
                                                scale=gcct[:, k:k + 1], bias=bcct[:, k:k + 1]), f"T1ev:{p}:{k}")
                for e in range(NZ):
                    if e == 0 and p > 0:
                        for jj in range(4):
                            u.wait(E, f"oD:{p - 1}:3:{jj}")
                    if e >= 2:
                        u.wait(E, f"gat:{p}:{e - 2}")
                    b = 4 * (e % 2)
                    u.wait(E, f"zc:{p}:{e}")
                    op(lambda b=b: E.activation(gsc[b + 1][:, :], gsc[b][:, :], AF.Sigmoid), f"sz:{p}:{e}")
                    u.wait(E, f"gc:{p}:{e}")
                    op(lambda b=b: E.activation(gsc[b + 3][:, :], gsc[b + 2][:, :], AF.Sigmoid), f"sg2:{p}:{e}")
                seq = [(2 * P2s + tci, j) for P2s in range(2) for tci in range(2) for j in range(4)]
                for k, (tcs, j) in enumerate(seq):
                    if j == 0 and tcs % 2 == 0:
                        u.wait(E, f"su:{k2g_m2(p, tcs // 2)}")
                    if k >= 2:
                        ptc, pj = seq[k - 2]
                        u.wait(E, f"x2:{p}:{ptc}" if pj == 3 else f"x2j:{p}:{ptc}:{pj}")
                    tci = tcs % 2
                    op(lambda p=p, tcs=tcs, tci=tci, j=j: E.activation(
                        gsc[4 * (j % 2)][:, :], pb[(4 if tci == 0 else 0) + j][:, :], AF.Identity,
                        scale=1.0, bias=coef[:, 12 + tcs:13 + tcs]), f"ev:{p}:{tcs}:{j}")
                u.wait(E, f"st2:{p}:0")
                for tc in range(2):
                    op(lambda p=p, tc=tc: E.activation(xn4[:, tc * D:(tc + 1) * D], xslot(p, tc), AF.Identity,
                                                       scale=coef[:, 16 + tc:17 + tc], bias=coef[:, 20 + tc:21 + tc]),
                       f"xnf:{p}:{tc}")
                for k in range(ND):
                    u.wait(E, f"T2Amm:{p}:{k}")
                    op(lambda k=k: E.activation(xnT[:, k * T:k * T + 256], pb[4 + k % 2][:, 0:256], AF.Identity),
                       f"T2evA:{p}:{k}")
                u.wait(E, f"st2:{p}:1")
                for tc in range(2, 4):
                    op(lambda p=p, tc=tc: E.activation(xn4[:, tc * D:(tc + 1) * D], xslot(p, tc), AF.Identity,
                                                       scale=coef[:, 16 + tc:17 + tc], bias=coef[:, 20 + tc:21 + tc]),
                       f"xnf:{p}:{tc}")
                for k in range(ND):
                    u.wait(E, f"T2Bmm:{p}:{k}")
                    op(lambda k=k: E.activation(xnT[:, k * T + 256:(k + 1) * T], pb[4 + k % 2][:, 256:512], AF.Identity),
                       f"T2evB:{p}:{k}")
                for h in range(NH):
                    if h == 0:
                        u.wait(E, f"gat:{p}:{NZ - 1}")
                    if h >= 2:
                        u.wait(E, f"actf:{p}:{h - 2}")
                    sgg = p * NSLAB + 4 * NZ + 2 * h
                    u.wait(E, f"su:{sgg}")
                    op(lambda h=h: E.activation(gsc[4 * (h % 2)][:, :], pb[2 * (h % 2)][:, :], AF.Silu,
                                                scale=sgct[:, h:h + 1]), f"sil:{p}:{h}")
                if p + 1 < NP:
                    u.wait(E, f"st1:{p + 1}")
                    for tc in range(4):
                        op(lambda p=p, tc=tc: E.activation(xn4[:, tc * D:(tc + 1) * D], xslot(p + 1, tc), AF.Identity,
                                                           scale=coef[:, 0 + tc:1 + tc], bias=coef[:, 4 + tc:5 + tc]),
                           f"xn1:{p + 1}:{tc}")

        gens = dict(gpsimd=gen_gpsimd, sync=gen_sync, tensor=gen_tensor,
                    vector=gen_vector2, scalar=gen_scalar)
        for name, g in gens.items():
            g(None)
        u.start_emit(sems)

        with nc.Block() as blk:
            @blk.gpsimd
            def _(e):
                for s in sems.values():
                    e.sem_clear(s)
                e.memset(m128[:, :], -128.0)

        with nc.Block() as blk:
            @blk.sync
            def _(e):
                gens["sync"](e)

            @blk.tensor
            def _(e):
                gens["tensor"](e)

            @blk.vector
            def _(e):
                gens["vector"](e)

            @blk.scalar
            def _(e):
                gens["scalar"](e)

            @blk.gpsimd
            def _(e):
                gens["gpsimd"](e)

    return nc


_BF = ml_dtypes.bfloat16


def _host_prep(inputs):
    x = np.ascontiguousarray(np.asarray(inputs["L0_attn_x"], dtype=np.float32).reshape(NCORES * TPC, D))
    Win = np.asarray(inputs["L0_attn_W_in"], dtype=np.float32)
    Wout = np.asarray(inputs["L0_attn_W_out"], dtype=np.float32)
    gamma = np.asarray(inputs["L0_attn_gamma"], dtype=np.float32)
    beta = np.asarray(inputs["L0_attn_beta"], dtype=np.float32)
    dt = np.asarray(inputs["L0_attn_dt"], dtype=np.float32)
    wq_g = np.asarray(inputs["L0_ffn_wq_gate"], dtype=np.int32)
    wq_u = np.asarray(inputs["L0_ffn_wq_up"], dtype=np.int32)
    wq_d = np.asarray(inputs["L0_ffn_wq_down"], dtype=np.int32)
    s_g = np.asarray(inputs["L0_ffn_s_gate"], dtype=np.float32)
    s_u = np.asarray(inputs["L0_ffn_s_up"], dtype=np.float32)
    s_d = np.asarray(inputs["L0_ffn_s_down"], dtype=np.float32)

    def rowslab(WT, nchunk):
        return np.ascontiguousarray(WT.reshape(ND, 128, nchunk, 128).transpose(2, 1, 0, 3).reshape(nchunk, 128, D))

    winT_h = np.ascontiguousarray(rowslab(Win.T, NE2)[ZIG])
    woutT_h = np.ascontiguousarray(Wout.T)
    wqg_h = rowslab(wq_g.T, NH)
    wqu_h = rowslab(wq_u.T, NH)
    wqd_h = np.ascontiguousarray(wq_d.T)
    sgc_h = np.ascontiguousarray(s_g.reshape(NH, 128).T)
    suc_h = np.ascontiguousarray(s_u.reshape(NH, 128).T)
    gcc_h = np.ascontiguousarray(gamma.reshape(ND, 128).T)
    bcc_h = np.ascontiguousarray(beta.reshape(ND, 128).T)
    sdb_h = np.ascontiguousarray(np.broadcast_to(s_d, (128, D)))
    dtr_h = np.ascontiguousarray(dt.reshape(1, 4096))
    idm_h = np.eye(128, dtype=_BF)

    shared = dict(winT=winT_h, woutT=woutT_h, wqg=wqg_h, wqu=wqu_h, wqd=wqd_h,
                  sgc=sgc_h, suc=suc_h, gcc=gcc_h, bcc=bcc_h, sdb=sdb_h,
                  dtr=dtr_h, idm=idm_h)
    in_maps = []
    for c in range(NCORES):
        m = dict(shared)
        m["x"] = np.ascontiguousarray(x[c * TPC:(c + 1) * TPC])
        in_maps.append(m)
    return in_maps


def kernel(**inputs):
    in_maps = _host_prep(inputs)
    nc = build_program()
    res = run_bass_kernel_spmd(nc, in_maps, core_ids=list(range(NCORES)))
    out = np.concatenate([np.asarray(r["out"], dtype=np.float32) for r in res.results], axis=0)
    return out.reshape(4, 8192, D)


# version 12
# speedup vs baseline: 1.0559x; 1.0559x over previous
import numpy as np
import ml_dtypes
import concourse.bass as bass
import concourse.mybir as mybir
from concourse.bass_utils import run_bass_kernel_spmd
from concourse.alu_op_type import AluOpType as OP

F32 = mybir.dt.float32
BF16 = mybir.dt.bfloat16
I32 = mybir.dt.int32
AF = mybir.ActivationFunctionType
AX = mybir.AxisListType.X

NCORES = 8
TPC = 4096
T = 512
NP = TPC // T
D = 2048
ND = 16
NZ = 32
NE2 = 64
H = 5632
NH = 44
EPS = 1e-5
NWB = 8

ZIG = np.empty(NE2, np.int64)
ZIG[0::2] = np.arange(NZ)
ZIG[1::2] = np.arange(NZ, NE2)


class U:

    def __init__(self):
        import collections
        self.cnt = collections.defaultdict(int)
        self.lab = {}
        self.emit = False
        self.sems = None

    def start_emit(self, sems):
        import collections
        self.emit = True
        self.sems = sems
        self.cnt = collections.defaultdict(int)

    def inc(self, sem, n, label=None, instr=None):
        self.cnt[sem] += n
        if label is not None:
            if not self.emit:
                self.lab[label] = (sem, self.cnt[sem])
            else:
                assert self.lab[label] == (sem, self.cnt[sem]), (label, self.lab[label], sem, self.cnt[sem])
        if instr is not None:
            instr.then_inc(self.sems[sem], n)

    def wait(self, E, label):
        if E is None:
            return
        sem, val = self.lab[label]
        E.wait_ge(self.sems[sem], val)

    def alias(self, name, existing):
        if not self.emit:
            self.lab[name] = self.lab[existing]
        else:
            assert self.lab[name] == self.lab[existing]


def build_program():
    nc = bass.Bass("TRN2", target_bir_lowering=False)

    x_in = nc.dram_tensor("x", [TPC, D], F32, kind="ExternalInput")
    out_hbm = nc.dram_tensor("out", [TPC, D], F32, kind="ExternalOutput")
    winT = nc.dram_tensor("winT", [NE2, 128, D], F32, kind="ExternalInput")
    woutT = nc.dram_tensor("woutT", [4096, D], F32, kind="ExternalInput")
    wqg = nc.dram_tensor("wqg", [NH, 128, D], I32, kind="ExternalInput")
    wqu = nc.dram_tensor("wqu", [NH, 128, D], I32, kind="ExternalInput")
    wqd = nc.dram_tensor("wqd", [H, D], I32, kind="ExternalInput")
    sgc_in = nc.dram_tensor("sgc", [128, NH], F32, kind="ExternalInput")
    suc_in = nc.dram_tensor("suc", [128, NH], F32, kind="ExternalInput")
    gcc_in = nc.dram_tensor("gcc", [128, ND], F32, kind="ExternalInput")
    bcc_in = nc.dram_tensor("bcc", [128, ND], F32, kind="ExternalInput")
    sdb_in = nc.dram_tensor("sdb", [128, D], F32, kind="ExternalInput")
    dtr_in = nc.dram_tensor("dtr", [1, 4096], F32, kind="ExternalInput")
    idm_in = nc.dram_tensor("idm", [128, 128], BF16, kind="ExternalInput")

    win_s = nc.dram_tensor("win_s", [NE2, 128, D], BF16, kind="Internal")
    wout_s = nc.dram_tensor("wout_s", [4096, D], BF16, kind="Internal")
    wg_s = nc.dram_tensor("wg_s", [NH, 128, D], BF16, kind="Internal")
    wu_s = nc.dram_tensor("wu_s", [NH, 128, D], BF16, kind="Internal")
    wd_s = nc.dram_tensor("wd_s", [H, D], BF16, kind="Internal")
    dtw_s = nc.dram_tensor("dtw_s", [1, 4096], F32, kind="Internal")

    u = U()

    import contextlib
    with contextlib.ExitStack() as st:
        def sb(name, shape, dt):
            return st.enter_context(nc.sbuf_tensor(name, shape, dt))

        xres = sb("xres", [128, 4 * D], F32)
        xb2 = sb("xb2", [128, D], F32)
        xb3 = sb("xb3", [128, D], F32)
        xn4 = sb("xn4", [128, 4 * D], BF16)
        xnT = sb("xnT", [128, ND * T], BF16)
        actb = sb("actb", [128, NH * T], BF16)
        wb = [sb(f"wb{i}", [128, D], BF16) for i in range(NWB)]
        gsc = [sb(f"gsc{i}", [128, T], F32) for i in range(8)]
        qi = [sb(f"qi{i}", [128, D], I32) for i in range(2)]
        qo = [sb(f"qo{i}", [128, D], BF16) for i in range(2)]
        sdbt = sb("sdbt", [128, D], F32)
        sgct = sb("sgct", [128, NH], F32)
        suct = sb("suct", [128, NH], F32)
        gcct = sb("gcct", [128, ND], F32)
        bcct = sb("bcct", [128, ND], F32)
        dtwt = sb("dtwt", [128, NZ], F32)
        idmt = sb("idmt", [128, 128], BF16)
        bnb = sb("bnb", [128, 24], F32)
        m128 = sb("m128", [128, 1], F32)
        mvb = sb("mvb", [128, 16], F32)
        coef = sb("coef", [128, 24], F32)
        ntmp = sb("ntmp", [128, 24], F32)

        pb = [st.enter_context(nc.psum_tensor(f"pb{i}", [128, T], F32)) for i in range(8)]

        sem_names = (["t", "v", "s", "g", "cst", "dt1", "dt2", "dt3", "od0", "od1", "od2", "od3"]
                     + [f"wb{i}" for i in range(NWB)]
                     + [f"qi{i}" for i in range(4)]
                     + [f"qs{i}" for i in range(4)]
                     + [f"x{i}" for i in range(8)])
        sems = {n: st.enter_context(nc.semaphore("s_" + n)) for n in sem_names}

        qi_f32 = [qi[0][:, :].bitcast(F32), qi[1][:, :].bitcast(F32)]
        ring_i32 = [qi[0][:, :], qi[1][:, :], xb2[:, :].bitcast(I32), xb3[:, :].bitcast(I32)]
        ring_f32 = [qi_f32[0], qi_f32[1], xb2[:, :], xb3[:, :]]

        def xslot(p, tc):
            if p % 2 == 0:
                return xres[:, tc * D:(tc + 1) * D]
            return [xb2[:, :], xb3[:, :], qi_f32[0], qi_f32[1]][tc]

        def xslot_j(p, tc, j):
            s = xslot(p, tc)
            if p % 2 == 0:
                return xres[:, tc * D + j * T: tc * D + (j + 1) * T]
            return [xb2, xb3, None, None][tc][:, j * T:(j + 1) * T] if tc < 2 else qi_f32[tc - 2][:, j * T:(j + 1) * T]

        dt_row = actb[0:1, 0:8192].bitcast(F32)
        dt_exp = actb[0:1, 8192:16384].bitcast(F32)

        def slab_list(p):
            out = []
            for e in range(NZ):
                out.append(("win", 2 * e))
                out.append(("win", 2 * e + 1))
            for P2 in range(2):
                for e in range(NZ):
                    out.append(("wout", e, P2))
            for h in range(NH):
                out.append(("wg", h))
                out.append(("wu", h))
            for P2 in range(2):
                for h in range(NH):
                    out.append(("wd", h, P2))
            return out

        NSLAB = len(slab_list(0))

        def k2g_m2(p, P2):
            return p * NSLAB + 2 * NZ + (P2 + 1) * NZ - 1

        ITEMS = []
        for e in range(NZ):
            ITEMS.append(("win", 2 * e))
            ITEMS.append(("win", 2 * e + 1))
            ITEMS.append(("wout", e))
        for h in range(NH):
            ITEMS.append(("wg", h))
            ITEMS.append(("wu", h))
            ITEMS.append(("wd", h))
        NIT = len(ITEMS)

        def item_of_slab(kind, idx):
            if kind == "win":
                return 3 * (idx // 2) + (idx % 2)
            if kind == "wout":
                return 3 * idx + 2
            base = 3 * NZ
            off = {"wg": 0, "wu": 1, "wd": 2}[kind]
            return base + 3 * idx + off

        def item_src(i_):
            kind, idx = ITEMS[i_]
            if kind == "win":
                return winT[idx]
            if kind == "wout":
                return woutT[idx * 128:(idx + 1) * 128, :]
            if kind == "wg":
                return wqg[idx]
            if kind == "wu":
                return wqu[idx]
            return wqd[idx * 128:(idx + 1) * 128, :]

        def p0_direct(kind):
            return kind in ("win", "wg", "wu")

        def slab_of_item(i_):
            kind, idx = ITEMS[i_]
            if kind == "win":
                return idx
            if kind == "wout":
                return 2 * NZ + idx
            if kind == "wg":
                return 4 * NZ + 2 * idx
            if kind == "wu":
                return 4 * NZ + 2 * idx + 1
            return 4 * NZ + 2 * NH + idx

        QOSLOT = {}
        _qct = 0
        for _i, (_k, _x) in enumerate(ITEMS):
            if not p0_direct(_k):
                QOSLOT[_i] = _qct % 2
                _qct += 1

        QOPREV = {}
        _last = {0: None, 1: None}
        for _i in sorted(QOSLOT):
            QOPREV[_i] = _last[QOSLOT[_i]]
            _last[QOSLOT[_i]] = _i

        def item_dst(i_):
            kind, idx = ITEMS[i_]
            if kind == "win":
                return win_s[idx]
            if kind == "wout":
                return wout_s[idx * 128:(idx + 1) * 128, :]
            if kind == "wg":
                return wg_s[idx]
            if kind == "wu":
                return wu_s[idx]
            return wd_s[idx * 128:(idx + 1) * 128, :]

        def newton_rsqrt(E, op, vp_ap, w, iters):
            A = ntmp[:, 0:w]
            B = ntmp[:, 4:4 + w]
            Dv = ntmp[:, 8:8 + w]
            Ev = ntmp[:, 12:12 + w]
            op(lambda: E.tensor_single_scalar(Dv, vp_ap, 0.4, OP.max))
            op(lambda: E.reciprocal(A, Dv))
            y, yn = A, B
            for _ in range(iters):
                op(lambda y=y: E.tensor_tensor(Dv, y, y, OP.mult))
                op(lambda: E.tensor_tensor(Ev, Dv, vp_ap, OP.mult))
                op(lambda: E.tensor_scalar(Ev, Ev, -0.5, 1.5, OP.mult, OP.add))
                op(lambda y=y, yn=yn: E.tensor_tensor(yn, y, Ev, OP.mult))
                y, yn = yn, y
            return y

        def wq_conv_sem(i_):
            return "s" if NH <= i_ < 2 * NH else "v"

        def gen_gpsimd(E):
            def dma(sem, out, in_, label=None, **kw):
                i = E.dma_start(out, in_, **kw) if E else None
                u.inc(sem, 16, label, i)

            for tc in range(4):
                dma(f"x{tc}", xslot(0, tc), x_in[tc * 128:(tc + 1) * 128, :], label=f"xin:0:{tc}")
            dma("cst", sgct[:, :], sgc_in[:, :])
            dma("cst", suct[:, :], suc_in[:, :])
            dma("cst", gcct[:, :], gcc_in[:, :])
            dma("cst", bcct[:, :], bcc_in[:, :])
            dma("cst", sdbt[:, :], sdb_in[:, :])
            dma("cst", idmt[:, :], idm_in[:, :], label="cst")
            dma("dt1", dt_row, dtr_in[:, :], label="dtin")
            def qld(i_):
                kind = ITEMS[i_][0]
                dst = ring_f32[i_ % 4] if kind in ("win", "wout") else ring_i32[i_ % 4]
                ii = E.dma_start(dst, item_src(i_)) if E else None
                u.inc(f"qi{i_ % 4}", 16, f"qld:{i_}", ii)

            for i_ in range(4):
                qld(i_)
            u.wait(E, "dtwrow")
            dma("dt2", dtw_s[:, :], dt_row, label="dtwst")
            u.wait(E, "dtwst")
            in_ap = dtw_s[:, :].rearrange("a (j p) -> p (a j)", p=128) if E else None
            i2 = E.dma_start(dtwt[:, 0:NZ], in_ap, allow_slow_non_contiguous=True) if E else None
            u.inc("dt3", 16, "dtwcols", i2)
            for i_ in range(NIT):
                if i_ >= 1:
                    u.wait(E, f"wqst:{i_ - 1}")
                    if E:
                        E.sem_inc(sems["g"], 1)
                    u.inc("g", 1, f"cf:{i_ - 1}", None)
                u.wait(E, f"wqts:{i_}")
                if p0_direct(ITEMS[i_][0]):
                    st_src = wb[slab_of_item(i_) % NWB][:, :]
                else:
                    st_src = qo[QOSLOT[i_]][:, :]
                ii = E.dma_start(item_dst(i_), st_src) if E else None
                u.inc(f"qs{i_ % 4}", 16, f"wqst:{i_}", ii)
                if i_ + 4 < NIT:
                    qld(i_ + 4)
            u.wait(E, f"wqst:{NIT - 1}")
            if E:
                E.sem_inc(sems["g"], 1)
            u.inc("g", 1, f"cf:{NIT - 1}", None)
            u.wait(E, f"wqts:{NIT - 2}")
            dma("x4", xslot(1, 0), x_in[T:T + 128, :], label="xin:1:0")
            u.wait(E, f"wqts:{NIT - 1}")
            dma("x5", xslot(1, 1), x_in[T + 128:T + 256, :], label="xin:1:1")
            u.wait(E, f"wqts:{NIT - 4}")
            dma("x6", xslot(1, 2), x_in[T + 256:T + 384, :], label="xin:1:2")
            u.wait(E, f"wqts:{NIT - 3}")
            dma("x7", xslot(1, 3), x_in[T + 384:T + 512, :], label="xin:1:3")

            for p in range(NP):
                for tc in range(4):
                    r0 = p * T + tc * 128
                    for j in range(4):
                        u.wait(E, f"fin:{p}:{tc}" if j == 3 else f"finj:{p}:{tc}:{j}")
                        ii = E.dma_start(out_hbm[r0:r0 + 128, j * T:(j + 1) * T],
                                         gsc[2 * j + 1][:, :]) if E else None
                        u.inc(f"od{j}", 16, f"oD:{p}:{tc}:{j}", ii)
                    if p + 2 < NP:
                        r1 = (p + 2) * T + tc * 128
                        ii = E.dma_start(xslot(p + 2, tc), x_in[r1:r1 + 128, :]) if E else None
                        u.inc(f"x{4 * (p % 2) + tc}", 16, f"xin:{p + 2}:{tc}", ii)
            if E:
                for jj in range(4):
                    E.wait_ge(sems[f"od{jj}"], u.cnt[f"od{jj}"])

        def gen_sync(E):
            g = 0
            NS0 = len(slab_list(0))
            for p in range(NP):
                for item in slab_list(p):
                    kind = item[0]
                    if p == 0 and p0_direct(kind):
                        g += 1
                        continue
                    if g >= NWB:
                        u.wait(E, f"su:{g - NWB}")
                        gprev = g - NWB
                        if gprev < NS0:
                            pk = slab_list(0)[gprev][0]
                            if p0_direct(pk):
                                u.wait(E, f"wqst:{item_of_slab(pk, slab_list(0)[gprev][1])}")
                    if p <= 1:
                        u.wait(E, f"cf:{item_of_slab(kind, item[1])}")
                    ii = None
                    if E:
                        slot = wb[g % NWB]
                        if kind == "win":
                            src = win_s[item[1]]
                        elif kind == "wout":
                            src = wout_s[item[1] * 128:(item[1] + 1) * 128, :]
                        elif kind == "wg":
                            src = wg_s[item[1]]
                        elif kind == "wu":
                            src = wu_s[item[1]]
                        else:
                            src = wd_s[item[1] * 128:(item[1] + 1) * 128, :]
                        ii = E.dma_start(slot[:, :], src)
                    u.inc(f"wb{g % NWB}", 16, f"sl:{g}", ii)
                    g += 1

        def gen_tensor(E):
            gslab = [0]

            def mm(out_ap, lhsT, rhs, start, stop, inc_label=None):
                i = E.matmul(out_ap, lhsT, rhs, start=start, stop=stop) if E else None
                if inc_label is not None:
                    u.inc("t", 1, inc_label, i)

            for p in range(NP):
                if p > 0:
                    u.wait(E, f"finj:{p - 1}:2:1")
                else:
                    u.wait(E, "cst")
                for k in range(ND):
                    if k >= 2:
                        u.wait(E, f"T1ev:{p}:{k - 2}")
                    for tc in range(4):
                        if k == 0:
                            u.wait(E, f"xn1:{p}:{tc}")
                        mm(pb[4 + k % 2][:, tc * 128:(tc + 1) * 128],
                           xn4[:, tc * D + k * 128: tc * D + (k + 1) * 128] if E else None,
                           idmt[:, :] if E else None, True, True,
                           inc_label=(f"T1mm:{p}:{k}" if tc == 3 else None))
                u.wait(E, f"T1ev:{p}:{ND - 1}")
                for e in range(NZ):
                    if p > 0 and e == 0:
                        u.wait(E, f"finj:{p - 1}:3:1")
                    if p > 0 and e == 1:
                        u.wait(E, f"fin:{p - 1}:3")
                    if e >= 2:
                        u.wait(E, f"gat:{p}:{e - 2}")
                    for half in range(2):
                        sg = gslab[0]
                        u.wait(E, f"sl:{sg}")
                        pz = pb[2 * (e % 2) + half]
                        for k in range(ND):
                            mm(pz[:, :], wb[sg % NWB][:, k * 128:(k + 1) * 128] if E else None,
                               xnT[:, k * T:(k + 1) * T] if E else None,
                               k == 0, k == ND - 1,
                               inc_label=(f"su:{sg}" if k == ND - 1 else None))
                        gslab[0] += 1
                for P2 in range(2):
                    for e in range(NZ):
                        sg = gslab[0]
                        u.wait(E, f"sl:{sg}")
                        if P2 == 0:
                            u.wait(E, f"gat:{p}:{e}")
                        for tci in range(2):
                            tc = 2 * P2 + tci
                            if e == 0:
                                if P2 == 0 and tci == 1:
                                    u.wait(E, f"gat:{p}:{NZ - 1}")
                            for j in range(4):
                                if e == 0 and P2 == 1:
                                    u.wait(E, f"ev:{p}:{tci}:{j}")
                                mm(pb[(4 if tci == 0 else 0) + j][:, 0:T],
                                   actb[:, e * T + tc * 128: e * T + (tc + 1) * 128] if E else None,
                                   wb[sg % NWB][:, j * T:(j + 1) * T] if E else None,
                                   e == 0, e == NZ - 1,
                                   inc_label=(f"su:{sg}" if (tci == 1 and j == 3) else None))
                        gslab[0] += 1
                u.wait(E, f"ev:{p}:2:1")
                for k in range(ND):
                    if k >= 2:
                        u.wait(E, f"T2evA:{p}:{k - 2}")
                    for tc in range(2):
                        if k == 0:
                            u.wait(E, f"xnf:{p}:{tc}")
                        mm(pb[4 + k % 2][:, tc * 128:(tc + 1) * 128],
                           xn4[:, tc * D + k * 128: tc * D + (k + 1) * 128] if E else None,
                           idmt[:, :] if E else None, True, True,
                           inc_label=(f"T2Amm:{p}:{k}" if tc == 1 else None))
                u.wait(E, f"T2evA:{p}:{ND - 1}")
                for k in range(ND):
                    if k >= 2:
                        u.wait(E, f"T2evB:{p}:{k - 2}")
                    for tc in range(2, 4):
                        if k == 0:
                            u.wait(E, f"xnf:{p}:{tc}")
                        mm(pb[4 + k % 2][:, tc * 128:(tc + 1) * 128],
                           xn4[:, tc * D + k * 128: tc * D + (k + 1) * 128] if E else None,
                           idmt[:, :] if E else None, True, True,
                           inc_label=(f"T2Bmm:{p}:{k}" if tc == 3 else None))
                u.wait(E, f"T2evB:{p}:{ND - 1}")
                for h in range(NH):
                    if h == 0:
                        u.wait(E, f"ev:{p}:3:1")
                    if h == 1:
                        u.wait(E, f"ev:{p}:3:3")
                    if h >= 2:
                        u.wait(E, f"actf:{p}:{h - 2}")
                    for half in range(2):
                        sg = gslab[0]
                        u.wait(E, f"sl:{sg}")
                        pz = pb[2 * (h % 2) + half]
                        for k in range(ND):
                            mm(pz[:, :], wb[sg % NWB][:, k * 128:(k + 1) * 128] if E else None,
                               xnT[:, k * T:(k + 1) * T] if E else None,
                               k == 0, k == ND - 1,
                               inc_label=(f"su:{sg}" if k == ND - 1 else None))
                        gslab[0] += 1
                for P2 in range(2):
                    if P2 == 0:
                        u.wait(E, f"actf:{p}:{NH - 1}")
                    for h in range(NH):
                        sg = gslab[0]
                        u.wait(E, f"sl:{sg}")
                        for tci in range(2):
                            tc = 2 * P2 + tci
                            for j in range(4):
                                if h == 0 and P2 == 1:
                                    u.wait(E, f"fin:{p}:{tci}" if j == 3 else f"finj:{p}:{tci}:{j}")
                                mm(pb[(4 if tci == 0 else 0) + j][:, 0:T],
                                   actb[:, h * T + tc * 128: h * T + (tc + 1) * 128] if E else None,
                                   wb[sg % NWB][:, j * T:(j + 1) * T] if E else None,
                                   h == 0, h == NH - 1,
                                   inc_label=(f"su:{sg}" if (tci == 1 and j == 3) else None))
                        gslab[0] += 1

        def gen_vector2(E):
            def op(fn, label=None):
                if E:
                    E.wait_ge(sems["v"], u.cnt["v"])
                i = fn() if E else None
                u.inc("v", 1, label, i)

            def conv_op(i_):
                u.wait(E, f"qld:{i_}")
                kind = ITEMS[i_][0]
                if p0_direct(kind):
                    g_ = slab_of_item(i_)
                    out_ap = wb[g_ % NWB][:, :]
                    if g_ >= NWB:
                        u.wait(E, f"su:{g_ - NWB}")
                        pk, px = slab_list(0)[g_ - NWB][0], slab_list(0)[g_ - NWB][1]
                        if p0_direct(pk):
                            u.wait(E, f"wqst:{item_of_slab(pk, px)}")
                else:
                    out_ap = qo[QOSLOT[i_]][:, :]
                    if QOPREV[i_] is not None:
                        u.wait(E, f"wqst:{QOPREV[i_]}")
                if kind in ("win", "wout"):
                    op(lambda: E.tensor_copy(out_ap, ring_f32[i_ % 4]), f"wqts:{i_}")
                elif kind in ("wg", "wu"):
                    op(lambda: E.tensor_single_scalar(out_ap, ring_i32[i_ % 4], -128.0, OP.add), f"wqts:{i_}")
                else:
                    op(lambda: E.scalar_tensor_tensor(out_ap, ring_i32[i_ % 4], -128.0,
                                                      sdbt[:, :], OP.add, OP.mult), f"wqts:{i_}")
                if p0_direct(kind):
                    u.alias(f"sl:{slab_of_item(i_)}", f"wqts:{i_}")

            u.wait(E, "dtin")
            op(lambda: E.tensor_reduce(ntmp[0:1, 0:1], dt_row, AX, OP.min), "dtmin")
            u.wait(E, "dtexp")
            op(lambda: E.tensor_reduce(ntmp[0:1, 1:2], dt_exp, AX, OP.add))
            op(lambda: E.reciprocal(ntmp[0:1, 2:3], ntmp[0:1, 1:2]))
            op(lambda: E.tensor_single_scalar(dt_row, dt_exp, ntmp[0:1, 2:3], OP.mult), "dtwrow")

            def ln_stats(p, tc, base):
                for j in range(4):
                    op(lambda p=p, tc=tc, j=j: E.bn_stats(bnb[:, 6 * j: 6 * (j + 1)], xslot_j(p, tc, j)))
                op(lambda tc=tc, base=base: E.bn_aggr(mvb[:, base + 2 * tc: base + 2 * tc + 2], bnb[:, 0:24]))

            def ln1_block(p):
                for tc in range(4):
                    u.wait(E, f"xin:{p}:{tc}")
                    ln_stats(p, tc, 0)
                w = 4
                mu = mvb[:, 0:8:2]
                v1 = mvb[:, 1:8:2]
                C = ntmp[:, 16:16 + w]
                Dv = ntmp[:, 8:8 + w]
                op(lambda: E.tensor_single_scalar(C, v1, EPS, OP.add))
                y1 = newton_rsqrt(E, op, C, w, 7)
                op(lambda y1=y1: E.tensor_single_scalar(coef[:, 8:12], y1, 1.0, OP.add))
                op(lambda y1=y1: E.tensor_tensor(Dv, mu, y1, OP.mult))
                op(lambda: E.tensor_single_scalar(coef[:, 12:16], Dv, -1.0, OP.mult))
                op(lambda: E.reciprocal(Dv, C))
                op(lambda: E.tensor_tensor(Dv, v1, Dv, OP.mult))
                op(lambda y1=y1: E.tensor_copy(coef[:, 0:4], y1))
                op(lambda: E.tensor_single_scalar(C, Dv, EPS, OP.add))
                y2 = newton_rsqrt(E, op, C, w, 4)
                op(lambda y2=y2: E.tensor_tensor(coef[:, 0:4], coef[:, 0:4], y2, OP.mult))
                op(lambda: E.tensor_tensor(Dv, mu, coef[:, 0:4], OP.mult))
                op(lambda: E.tensor_single_scalar(coef[:, 4:8], Dv, -1.0, OP.mult), f"st1:{p}")

            for p in range(NP):
                if p == 0:
                    ln1_block(0)
                    u.wait(E, "cst")
                    for i_ in range(6):
                        conv_op(i_)
                for e in range(NZ):
                    sgg = p * NSLAB + 2 * e + 1
                    u.wait(E, f"su:{sgg}")
                    if p == 0 and e == 0:
                        u.wait(E, "dtwcols")
                    b = 4 * (e % 2)
                    pz, pg = pb[2 * (e % 2)], pb[2 * (e % 2) + 1]
                    op(lambda pz=pz, b=b: E.tensor_scalar(gsc[b][:, :], pz[:, :], 15.0, -15.0, OP.min, OP.max), f"zc:{p}:{e}")
                    op(lambda pg=pg, b=b: E.tensor_scalar(gsc[b + 2][:, :], pg[:, :], 15.0, -15.0, OP.min, OP.max), f"gc:{p}:{e}")
                    u.wait(E, f"sg2:{p}:{e}")
                    op(lambda b=b: E.tensor_tensor(gsc[b][:, :], gsc[b + 2][:, :], gsc[b + 3][:, :], OP.mult))
                    op(lambda pz=pz, b=b: E.tensor_tensor(gsc[b + 2][:, :], pz[:, :], gsc[b + 1][:, :], OP.mult))
                    op(lambda b=b, e=e: E.scalar_tensor_tensor(actb[:, e * T:(e + 1) * T], gsc[b + 2][:, :],
                                                               dtwt[:, e:e + 1], gsc[b][:, :], OP.mult, OP.mult), f"gat:{p}:{e}")
                    if p == 0 and e + 2 < NZ:
                        for i_ in range(3 * (e + 2), 3 * (e + 2) + 3):
                            conv_op(i_)
                for P2 in range(2):
                    last_slab = p * NSLAB + 2 * NZ + (P2 + 1) * NZ - 1
                    u.wait(E, f"su:{last_slab}")
                    for tci in range(2):
                        tc = 2 * P2 + tci
                        for j in range(4):
                            u.wait(E, f"ev:{p}:{tc}:{j}")
                            op(lambda p=p, tc=tc, j=j: E.scalar_tensor_tensor(
                                xslot_j(p, tc, j), xslot_j(p, tc, j), coef[:, 8 + tc:9 + tc],
                                gsc[4 * (j % 2)][:, :], OP.mult, OP.add),
                                f"x2:{p}:{tc}" if j == 3 else f"x2j:{p}:{tc}:{j}")
                        ln_stats(p, tc, 8)
                    w = 2
                    mu = mvb[:, 8 + 4 * P2: 8 + 4 * P2 + 3:2]
                    v1 = mvb[:, 9 + 4 * P2: 9 + 4 * P2 + 3:2]
                    C = ntmp[:, 16:16 + w]
                    Dv = ntmp[:, 8:8 + w]
                    op(lambda v1=v1, C=C: E.tensor_single_scalar(C, v1, EPS, OP.add))
                    yf = newton_rsqrt(E, op, C, w, 7)
                    op(lambda yf=yf, P2=P2: E.tensor_single_scalar(coef[:, 16 + 2 * P2:18 + 2 * P2], yf, 1.0, OP.mult))
                    op(lambda mu=mu, yf=yf, Dv=Dv: E.tensor_tensor(Dv, mu, yf, OP.mult))
                    op(lambda Dv=Dv, P2=P2: E.tensor_single_scalar(coef[:, 20 + 2 * P2:22 + 2 * P2], Dv, -1.0, OP.mult), f"st2:{p}:{P2}")
                    if p == 0 and P2 == 1:
                        for k in range(9):
                            conv_op(3 * NZ + k)
                for h in range(NH):
                    sgu = p * NSLAB + 4 * NZ + 2 * h + 1
                    u.wait(E, f"su:{sgu}")
                    u.wait(E, f"sil:{p}:{h}")
                    op(lambda h=h: E.scalar_tensor_tensor(actb[:, h * T:(h + 1) * T],
                                                          pb[2 * (h % 2) + 1][:, :], suct[:, h:h + 1],
                                                          gsc[4 * (h % 2)][:, :], OP.mult, OP.mult), f"actf:{p}:{h}")
                    if p == 0 and h <= 40:
                        for i_ in range(3 * NZ + 9 + 3 * h, 3 * NZ + 9 + 3 * h + 3):
                            conv_op(i_)
                if p + 1 < NP:
                    ln1_block(p + 1)
                for P2 in range(2):
                    last_slab = p * NSLAB + 4 * NZ + 2 * NH + (P2 + 1) * NH - 1
                    u.wait(E, f"su:{last_slab}")
                    for tci in range(2):
                        tc = 2 * P2 + tci
                        for j in range(4):
                            if tc > 0:
                                u.wait(E, f"oD:{p}:{tc - 1}:{j}")
                            elif p > 0:
                                u.wait(E, f"oD:{p - 1}:3:{j}")
                            op(lambda p=p, tc=tc, tci=tci, j=j: E.tensor_tensor(
                                gsc[2 * j + 1][:, :], xslot_j(p, tc, j),
                                pb[(4 if tci == 0 else 0) + j][:, :], OP.add),
                                f"fin:{p}:{tc}" if j == 3 else f"finj:{p}:{tc}:{j}")

        def gen_scalar(E):
            def op(fn, label=None):
                if E:
                    E.wait_ge(sems["s"], u.cnt["s"])
                i = fn() if E else None
                u.inc("s", 1, label, i)

            u.wait(E, "dtmin")
            op(lambda: E.activation(dt_exp, dt_row, AF.Exp, scale=-1.0, bias=ntmp[0:1, 0:1]), "dtexp")
            for p in range(NP):
                if p == 0:
                    u.wait(E, "st1:0")
                    for tc in range(4):
                        op(lambda tc=tc: E.activation(xn4[:, tc * D:(tc + 1) * D], xslot(0, tc), AF.Identity,
                                                      scale=coef[:, 0 + tc:1 + tc], bias=coef[:, 4 + tc:5 + tc]),
                           f"xn1:0:{tc}")
                for k in range(ND):
                    u.wait(E, f"T1mm:{p}:{k}")
                    if p == 0 and k == 0:
                        u.wait(E, "cst")
                    op(lambda k=k: E.activation(xnT[:, k * T:(k + 1) * T], pb[4 + k % 2][:, :], AF.Identity,
                                                scale=gcct[:, k:k + 1], bias=bcct[:, k:k + 1]), f"T1ev:{p}:{k}")
                for e in range(NZ):
                    if e == 0 and p > 0:
                        for jj in range(4):
                            u.wait(E, f"oD:{p - 1}:3:{jj}")
                    if e >= 2:
                        u.wait(E, f"gat:{p}:{e - 2}")
                    b = 4 * (e % 2)
                    u.wait(E, f"zc:{p}:{e}")
                    op(lambda b=b: E.activation(gsc[b + 1][:, :], gsc[b][:, :], AF.Sigmoid), f"sz:{p}:{e}")
                    u.wait(E, f"gc:{p}:{e}")
                    op(lambda b=b: E.activation(gsc[b + 3][:, :], gsc[b + 2][:, :], AF.Sigmoid), f"sg2:{p}:{e}")
                seq = [(2 * P2s + tci, j) for P2s in range(2) for tci in range(2) for j in range(4)]
                for k, (tcs, j) in enumerate(seq):
                    if j == 0 and tcs % 2 == 0:
                        u.wait(E, f"su:{k2g_m2(p, tcs // 2)}")
                    if k >= 2:
                        ptc, pj = seq[k - 2]
                        u.wait(E, f"x2:{p}:{ptc}" if pj == 3 else f"x2j:{p}:{ptc}:{pj}")
                    tci = tcs % 2
                    op(lambda p=p, tcs=tcs, tci=tci, j=j: E.activation(
                        gsc[4 * (j % 2)][:, :], pb[(4 if tci == 0 else 0) + j][:, :], AF.Identity,
                        scale=1.0, bias=coef[:, 12 + tcs:13 + tcs]), f"ev:{p}:{tcs}:{j}")
                u.wait(E, f"st2:{p}:0")
                for tc in range(2):
                    op(lambda p=p, tc=tc: E.activation(xn4[:, tc * D:(tc + 1) * D], xslot(p, tc), AF.Identity,
                                                       scale=coef[:, 16 + tc:17 + tc], bias=coef[:, 20 + tc:21 + tc]),
                       f"xnf:{p}:{tc}")
                for k in range(ND):
                    u.wait(E, f"T2Amm:{p}:{k}")
                    op(lambda k=k: E.activation(xnT[:, k * T:k * T + 256], pb[4 + k % 2][:, 0:256], AF.Identity),
                       f"T2evA:{p}:{k}")
                u.wait(E, f"st2:{p}:1")
                for tc in range(2, 4):
                    op(lambda p=p, tc=tc: E.activation(xn4[:, tc * D:(tc + 1) * D], xslot(p, tc), AF.Identity,
                                                       scale=coef[:, 16 + tc:17 + tc], bias=coef[:, 20 + tc:21 + tc]),
                       f"xnf:{p}:{tc}")
                for k in range(ND):
                    u.wait(E, f"T2Bmm:{p}:{k}")
                    op(lambda k=k: E.activation(xnT[:, k * T + 256:(k + 1) * T], pb[4 + k % 2][:, 256:512], AF.Identity),
                       f"T2evB:{p}:{k}")
                for h in range(NH):
                    if h == 0:
                        u.wait(E, f"gat:{p}:{NZ - 1}")
                    if h >= 2:
                        u.wait(E, f"actf:{p}:{h - 2}")
                    sgg = p * NSLAB + 4 * NZ + 2 * h
                    u.wait(E, f"su:{sgg}")
                    op(lambda h=h: E.activation(gsc[4 * (h % 2)][:, :], pb[2 * (h % 2)][:, :], AF.Silu,
                                                scale=sgct[:, h:h + 1]), f"sil:{p}:{h}")
                if p + 1 < NP:
                    u.wait(E, f"st1:{p + 1}")
                    for tc in range(4):
                        op(lambda p=p, tc=tc: E.activation(xn4[:, tc * D:(tc + 1) * D], xslot(p + 1, tc), AF.Identity,
                                                           scale=coef[:, 0 + tc:1 + tc], bias=coef[:, 4 + tc:5 + tc]),
                           f"xn1:{p + 1}:{tc}")

        gens = dict(gpsimd=gen_gpsimd, sync=gen_sync, tensor=gen_tensor,
                    vector=gen_vector2, scalar=gen_scalar)
        for name, g in gens.items():
            g(None)
        u.start_emit(sems)

        with nc.Block() as blk:
            @blk.gpsimd
            def _(e):
                for s in sems.values():
                    e.sem_clear(s)
                e.memset(m128[:, :], -128.0)

        with nc.Block() as blk:
            @blk.sync
            def _(e):
                gens["sync"](e)

            @blk.tensor
            def _(e):
                gens["tensor"](e)

            @blk.vector
            def _(e):
                gens["vector"](e)

            @blk.scalar
            def _(e):
                gens["scalar"](e)

            @blk.gpsimd
            def _(e):
                gens["gpsimd"](e)

    return nc


_BF = ml_dtypes.bfloat16


def _host_prep(inputs):
    x = np.ascontiguousarray(np.asarray(inputs["L0_attn_x"], dtype=np.float32).reshape(NCORES * TPC, D))
    Win = np.asarray(inputs["L0_attn_W_in"], dtype=np.float32)
    Wout = np.asarray(inputs["L0_attn_W_out"], dtype=np.float32)
    gamma = np.asarray(inputs["L0_attn_gamma"], dtype=np.float32)
    beta = np.asarray(inputs["L0_attn_beta"], dtype=np.float32)
    dt = np.asarray(inputs["L0_attn_dt"], dtype=np.float32)
    wq_g = np.asarray(inputs["L0_ffn_wq_gate"], dtype=np.int32)
    wq_u = np.asarray(inputs["L0_ffn_wq_up"], dtype=np.int32)
    wq_d = np.asarray(inputs["L0_ffn_wq_down"], dtype=np.int32)
    s_g = np.asarray(inputs["L0_ffn_s_gate"], dtype=np.float32)
    s_u = np.asarray(inputs["L0_ffn_s_up"], dtype=np.float32)
    s_d = np.asarray(inputs["L0_ffn_s_down"], dtype=np.float32)

    def rowslab(WT, nchunk):
        return np.ascontiguousarray(WT.reshape(ND, 128, nchunk, 128).transpose(2, 1, 0, 3).reshape(nchunk, 128, D))

    winT_h = np.ascontiguousarray(rowslab(Win.T, NE2)[ZIG])
    woutT_h = np.ascontiguousarray(Wout.T)
    wqg_h = rowslab(wq_g.T, NH)
    wqu_h = rowslab(wq_u.T, NH)
    wqd_h = np.ascontiguousarray(wq_d.T)
    sgc_h = np.ascontiguousarray(s_g.reshape(NH, 128).T)
    suc_h = np.ascontiguousarray(s_u.reshape(NH, 128).T)
    gcc_h = np.ascontiguousarray(gamma.reshape(ND, 128).T)
    bcc_h = np.ascontiguousarray(beta.reshape(ND, 128).T)
    sdb_h = np.ascontiguousarray(np.broadcast_to(s_d, (128, D)))
    dtr_h = np.ascontiguousarray(dt.reshape(1, 4096))
    idm_h = np.eye(128, dtype=_BF)

    shared = dict(winT=winT_h, woutT=woutT_h, wqg=wqg_h, wqu=wqu_h, wqd=wqd_h,
                  sgc=sgc_h, suc=suc_h, gcc=gcc_h, bcc=bcc_h, sdb=sdb_h,
                  dtr=dtr_h, idm=idm_h)
    in_maps = []
    for c in range(NCORES):
        m = dict(shared)
        m["x"] = np.ascontiguousarray(x[c * TPC:(c + 1) * TPC])
        in_maps.append(m)
    return in_maps


def kernel(**inputs):
    in_maps = _host_prep(inputs)
    nc = build_program()
    res = run_bass_kernel_spmd(nc, in_maps, core_ids=list(range(NCORES)))
    out = np.concatenate([np.asarray(r["out"], dtype=np.float32) for r in res.results], axis=0)
    return out.reshape(4, 8192, D)


# version 13
# speedup vs baseline: 1.1537x; 1.0926x over previous
import numpy as np
import ml_dtypes
import concourse.bass as bass
import concourse.mybir as mybir
from concourse.bass_utils import run_bass_kernel_spmd
from concourse.alu_op_type import AluOpType as OP

F32 = mybir.dt.float32
BF16 = mybir.dt.bfloat16
I32 = mybir.dt.int32
AF = mybir.ActivationFunctionType
AX = mybir.AxisListType.X

NCORES = 8
TPC = 4096
T = 512
NP = TPC // T
D = 2048
ND = 16
NZ = 32
NE2 = 64
H = 5632
NH = 44
EPS = 1e-5
NWB = 8

ZIG = np.empty(NE2, np.int64)
ZIG[0::2] = np.arange(NZ)
ZIG[1::2] = np.arange(NZ, NE2)


class U:

    def __init__(self):
        import collections
        self.cnt = collections.defaultdict(int)
        self.lab = {}
        self.emit = False
        self.sems = None

    def start_emit(self, sems):
        import collections
        self.emit = True
        self.sems = sems
        self.cnt = collections.defaultdict(int)

    def inc(self, sem, n, label=None, instr=None):
        self.cnt[sem] += n
        if label is not None:
            if not self.emit:
                self.lab[label] = (sem, self.cnt[sem])
            else:
                assert self.lab[label] == (sem, self.cnt[sem]), (label, self.lab[label], sem, self.cnt[sem])
        if instr is not None:
            instr.then_inc(self.sems[sem], n)

    def wait(self, E, label):
        if E is None:
            return
        sem, val = self.lab[label]
        E.wait_ge(self.sems[sem], val)

    def alias(self, name, existing):
        if not self.emit:
            self.lab[name] = self.lab[existing]
        else:
            assert self.lab[name] == self.lab[existing]


def build_program():
    nc = bass.Bass("TRN2", target_bir_lowering=False)

    x_in = nc.dram_tensor("x", [TPC, D], F32, kind="ExternalInput")
    out_hbm = nc.dram_tensor("out", [TPC, D], F32, kind="ExternalOutput")
    winT = nc.dram_tensor("winT", [NE2, 128, D], F32, kind="ExternalInput")
    woutT = nc.dram_tensor("woutT", [4096, D], F32, kind="ExternalInput")
    wqg = nc.dram_tensor("wqg", [NH, 128, D], I32, kind="ExternalInput")
    wqu = nc.dram_tensor("wqu", [NH, 128, D], I32, kind="ExternalInput")
    wqd = nc.dram_tensor("wqd", [H, D], I32, kind="ExternalInput")
    sgc_in = nc.dram_tensor("sgc", [128, NH], F32, kind="ExternalInput")
    suc_in = nc.dram_tensor("suc", [128, NH], F32, kind="ExternalInput")
    gcc_in = nc.dram_tensor("gcc", [128, ND], F32, kind="ExternalInput")
    bcc_in = nc.dram_tensor("bcc", [128, ND], F32, kind="ExternalInput")
    sdb_in = nc.dram_tensor("sdb", [128, D], F32, kind="ExternalInput")
    dtr_in = nc.dram_tensor("dtr", [1, 4096], F32, kind="ExternalInput")
    idm_in = nc.dram_tensor("idm", [128, 128], BF16, kind="ExternalInput")

    win_s = nc.dram_tensor("win_s", [NE2, 128, D], BF16, kind="Internal")
    wout_s = nc.dram_tensor("wout_s", [4096, D], BF16, kind="Internal")
    wg_s = nc.dram_tensor("wg_s", [NH, 128, D], BF16, kind="Internal")
    wu_s = nc.dram_tensor("wu_s", [NH, 128, D], BF16, kind="Internal")
    wd_s = nc.dram_tensor("wd_s", [H, D], BF16, kind="Internal")
    dtw_s = nc.dram_tensor("dtw_s", [1, 4096], F32, kind="Internal")

    u = U()

    import contextlib
    with contextlib.ExitStack() as st:
        def sb(name, shape, dt):
            return st.enter_context(nc.sbuf_tensor(name, shape, dt))

        xres = sb("xres", [128, 4 * D], F32)
        xb2 = sb("xb2", [128, D], F32)
        xb3 = sb("xb3", [128, D], F32)
        xn4 = sb("xn4", [128, 4 * D], BF16)
        xnT = sb("xnT", [128, ND * T], BF16)
        actb = sb("actb", [128, NH * T], BF16)
        wb = [sb(f"wb{i}", [128, D], BF16) for i in range(NWB)]
        gsc = [sb(f"gsc{i}", [128, T], F32) for i in range(8)]
        qi = [sb(f"qi{i}", [128, D], I32) for i in range(2)]
        qo = [sb(f"qo{i}", [128, D], BF16) for i in range(2)]
        sdbt = sb("sdbt", [128, D], F32)
        sgct = sb("sgct", [128, NH], F32)
        suct = sb("suct", [128, NH], F32)
        gcct = sb("gcct", [128, ND], F32)
        bcct = sb("bcct", [128, ND], F32)
        dtwt = sb("dtwt", [128, NZ], F32)
        idmt = sb("idmt", [128, 128], BF16)
        bnb = sb("bnb", [128, 24], F32)
        m128 = sb("m128", [128, 1], F32)
        mvb = sb("mvb", [128, 16], F32)
        coef = sb("coef", [128, 24], F32)
        ntmp = sb("ntmp", [128, 24], F32)

        pb = [st.enter_context(nc.psum_tensor(f"pb{i}", [128, T], F32)) for i in range(8)]

        sem_names = (["t", "v", "s", "g", "cst", "dt1", "dt2", "dt3", "od0", "od1", "od2", "od3"]
                     + [f"wb{i}" for i in range(NWB)]
                     + [f"qi{i}" for i in range(4)]
                     + [f"qs{i}" for i in range(4)]
                     + [f"x{i}" for i in range(8)])
        sems = {n: st.enter_context(nc.semaphore("s_" + n)) for n in sem_names}

        qi_f32 = [qi[0][:, :].bitcast(F32), qi[1][:, :].bitcast(F32)]
        ring_i32 = [qi[0][:, :], qi[1][:, :], xb2[:, :].bitcast(I32), xb3[:, :].bitcast(I32)]
        ring_f32 = [qi_f32[0], qi_f32[1], xb2[:, :], xb3[:, :]]

        def xslot(p, tc):
            if p % 2 == 0:
                return xres[:, tc * D:(tc + 1) * D]
            return [xb2[:, :], xb3[:, :], qi_f32[0], qi_f32[1]][tc]

        def xslot_j(p, tc, j):
            s = xslot(p, tc)
            if p % 2 == 0:
                return xres[:, tc * D + j * T: tc * D + (j + 1) * T]
            return [xb2, xb3, None, None][tc][:, j * T:(j + 1) * T] if tc < 2 else qi_f32[tc - 2][:, j * T:(j + 1) * T]

        dt_row = actb[0:1, 0:8192].bitcast(F32)
        dt_exp = actb[0:1, 8192:16384].bitcast(F32)

        def slab_list(p):
            out = []
            for e in range(NZ):
                out.append(("win", 2 * e))
                out.append(("win", 2 * e + 1))
            for P2 in range(2):
                for e in range(NZ):
                    out.append(("wout", e, P2))
            for h in range(NH):
                out.append(("wg", h))
                out.append(("wu", h))
            for P2 in range(2):
                for h in range(NH):
                    out.append(("wd", h, P2))
            return out

        NSLAB = len(slab_list(0))

        def k2g_m2(p, P2):
            return p * NSLAB + 2 * NZ + (P2 + 1) * NZ - 1

        ITEMS = []
        for e in range(NZ):
            ITEMS.append(("win", 2 * e))
            ITEMS.append(("win", 2 * e + 1))
            ITEMS.append(("wout", e))
        for h in range(NH):
            ITEMS.append(("wg", h))
            ITEMS.append(("wu", h))
            ITEMS.append(("wd", h))
        NIT = len(ITEMS)

        def item_of_slab(kind, idx):
            if kind == "win":
                return 3 * (idx // 2) + (idx % 2)
            if kind == "wout":
                return 3 * idx + 2
            base = 3 * NZ
            off = {"wg": 0, "wu": 1, "wd": 2}[kind]
            return base + 3 * idx + off

        def item_src(i_):
            kind, idx = ITEMS[i_]
            if kind == "win":
                return winT[idx]
            if kind == "wout":
                return woutT[idx * 128:(idx + 1) * 128, :]
            if kind == "wg":
                return wqg[idx]
            if kind == "wu":
                return wqu[idx]
            return wqd[idx * 128:(idx + 1) * 128, :]

        def p0_direct(kind):
            return kind in ("win", "wg", "wu")

        def slab_of_item(i_):
            kind, idx = ITEMS[i_]
            if kind == "win":
                return idx
            if kind == "wout":
                return 2 * NZ + idx
            if kind == "wg":
                return 4 * NZ + 2 * idx
            if kind == "wu":
                return 4 * NZ + 2 * idx + 1
            return 4 * NZ + 2 * NH + idx

        QOSLOT = {}
        _qct = 0
        for _i, (_k, _x) in enumerate(ITEMS):
            if not p0_direct(_k):
                QOSLOT[_i] = _qct % 2
                _qct += 1

        QOPREV = {}
        _last = {0: None, 1: None}
        for _i in sorted(QOSLOT):
            QOPREV[_i] = _last[QOSLOT[_i]]
            _last[QOSLOT[_i]] = _i

        def item_dst(i_):
            kind, idx = ITEMS[i_]
            if kind == "win":
                return win_s[idx]
            if kind == "wout":
                return wout_s[idx * 128:(idx + 1) * 128, :]
            if kind == "wg":
                return wg_s[idx]
            if kind == "wu":
                return wu_s[idx]
            return wd_s[idx * 128:(idx + 1) * 128, :]

        def newton_rsqrt(E, op, vp_ap, w, iters):
            A = ntmp[:, 0:w]
            B = ntmp[:, 4:4 + w]
            Dv = ntmp[:, 8:8 + w]
            Ev = ntmp[:, 12:12 + w]
            op(lambda: E.tensor_single_scalar(Dv, vp_ap, 0.4, OP.max))
            op(lambda: E.reciprocal(A, Dv))
            y, yn = A, B
            for _ in range(iters):
                op(lambda y=y: E.tensor_tensor(Dv, y, y, OP.mult))
                op(lambda: E.tensor_tensor(Ev, Dv, vp_ap, OP.mult))
                op(lambda: E.tensor_scalar(Ev, Ev, -0.5, 1.5, OP.mult, OP.add))
                op(lambda y=y, yn=yn: E.tensor_tensor(yn, y, Ev, OP.mult))
                y, yn = yn, y
            return y

        def wq_conv_sem(i_):
            return "s" if NH <= i_ < 2 * NH else "v"

        def gen_gpsimd(E):
            def dma(sem, out, in_, label=None, **kw):
                i = E.dma_start(out, in_, **kw) if E else None
                u.inc(sem, 16, label, i)

            for tc in range(4):
                dma(f"x{tc}", xslot(0, tc), x_in[tc * 128:(tc + 1) * 128, :], label=f"xin:0:{tc}")
            dma("cst", sgct[:, :], sgc_in[:, :])
            dma("cst", suct[:, :], suc_in[:, :])
            dma("cst", gcct[:, :], gcc_in[:, :])
            dma("cst", bcct[:, :], bcc_in[:, :])
            dma("cst", sdbt[:, :], sdb_in[:, :])
            dma("cst", idmt[:, :], idm_in[:, :], label="cst")
            dma("dt1", dt_row, dtr_in[:, :], label="dtin")
            def qld(i_):
                kind = ITEMS[i_][0]
                dst = ring_f32[i_ % 4] if kind in ("win", "wout") else ring_i32[i_ % 4]
                ii = E.dma_start(dst, item_src(i_)) if E else None
                u.inc(f"qi{i_ % 4}", 16, f"qld:{i_}", ii)

            for i_ in range(4):
                qld(i_)
            u.wait(E, "dtwrow")
            dma("dt2", dtw_s[:, :], dt_row, label="dtwst")
            u.wait(E, "dtwst")
            in_ap = dtw_s[:, :].rearrange("a (j p) -> p (a j)", p=128) if E else None
            i2 = E.dma_start(dtwt[:, 0:NZ], in_ap, allow_slow_non_contiguous=True) if E else None
            u.inc("dt3", 16, "dtwcols", i2)
            for i_ in range(NIT):
                if i_ >= 1:
                    u.wait(E, f"wqst:{i_ - 1}")
                    if E:
                        E.sem_inc(sems["g"], 1)
                    u.inc("g", 1, f"cf:{i_ - 1}", None)
                u.wait(E, f"wqts:{i_}")
                if p0_direct(ITEMS[i_][0]):
                    st_src = wb[slab_of_item(i_) % NWB][:, :]
                else:
                    st_src = qo[QOSLOT[i_]][:, :]
                ii = E.dma_start(item_dst(i_), st_src) if E else None
                u.inc(f"qs{i_ % 4}", 16, f"wqst:{i_}", ii)
                if i_ + 4 < NIT:
                    qld(i_ + 4)
            u.wait(E, f"wqst:{NIT - 1}")
            if E:
                E.sem_inc(sems["g"], 1)
            u.inc("g", 1, f"cf:{NIT - 1}", None)
            u.wait(E, f"wqts:{NIT - 2}")
            dma("x4", xslot(1, 0), x_in[T:T + 128, :], label="xin:1:0")
            u.wait(E, f"wqts:{NIT - 1}")
            dma("x5", xslot(1, 1), x_in[T + 128:T + 256, :], label="xin:1:1")
            u.wait(E, f"wqts:{NIT - 4}")
            dma("x6", xslot(1, 2), x_in[T + 256:T + 384, :], label="xin:1:2")
            u.wait(E, f"wqts:{NIT - 3}")
            dma("x7", xslot(1, 3), x_in[T + 384:T + 512, :], label="xin:1:3")

            for p in range(NP):
                for tc in range(4):
                    r0 = p * T + tc * 128
                    for j in range(4):
                        u.wait(E, f"fin:{p}:{tc}" if j == 3 else f"finj:{p}:{tc}:{j}")
                        ii = E.dma_start(out_hbm[r0:r0 + 128, j * T:(j + 1) * T],
                                         gsc[2 * j + 1][:, :]) if E else None
                        u.inc(f"od{j}", 16, f"oD:{p}:{tc}:{j}", ii)
                    if p + 2 < NP:
                        r1 = (p + 2) * T + tc * 128
                        ii = E.dma_start(xslot(p + 2, tc), x_in[r1:r1 + 128, :]) if E else None
                        u.inc(f"x{4 * (p % 2) + tc}", 16, f"xin:{p + 2}:{tc}", ii)
            if E:
                for jj in range(4):
                    E.wait_ge(sems[f"od{jj}"], u.cnt[f"od{jj}"])

        def gen_sync(E):
            g = 0
            NS0 = len(slab_list(0))
            for p in range(NP):
                for item in slab_list(p):
                    kind = item[0]
                    if p == 0 and p0_direct(kind):
                        g += 1
                        continue
                    if g >= NWB:
                        u.wait(E, f"su:{g - NWB}")
                        gprev = g - NWB
                        if gprev < NS0:
                            pk = slab_list(0)[gprev][0]
                            if p0_direct(pk):
                                u.wait(E, f"wqst:{item_of_slab(pk, slab_list(0)[gprev][1])}")
                    if p <= 1:
                        u.wait(E, f"cf:{item_of_slab(kind, item[1])}")
                    ii = None
                    if E:
                        slot = wb[g % NWB]
                        if kind == "win":
                            src = win_s[item[1]]
                        elif kind == "wout":
                            src = wout_s[item[1] * 128:(item[1] + 1) * 128, :]
                        elif kind == "wg":
                            src = wg_s[item[1]]
                        elif kind == "wu":
                            src = wu_s[item[1]]
                        else:
                            src = wd_s[item[1] * 128:(item[1] + 1) * 128, :]
                        ii = E.dma_start(slot[:, :], src)
                    u.inc(f"wb{g % NWB}", 16, f"sl:{g}", ii)
                    g += 1

        def gen_tensor(E):
            gslab = [0]

            def mm(out_ap, lhsT, rhs, start, stop, inc_label=None):
                i = E.matmul(out_ap, lhsT, rhs, start=start, stop=stop) if E else None
                if inc_label is not None:
                    u.inc("t", 1, inc_label, i)

            for p in range(NP):
                if p > 0:
                    u.wait(E, f"finj:{p - 1}:2:1")
                else:
                    u.wait(E, "cst")
                for k in range(ND):
                    if k >= 2:
                        u.wait(E, f"T1ev:{p}:{k - 2}")
                    for tc in range(4):
                        if k == 0:
                            u.wait(E, f"xn1:{p}:{tc}")
                        mm(pb[4 + k % 2][:, tc * 128:(tc + 1) * 128],
                           xn4[:, tc * D + k * 128: tc * D + (k + 1) * 128] if E else None,
                           idmt[:, :] if E else None, True, True,
                           inc_label=(f"T1mm:{p}:{k}" if tc == 3 else None))
                u.wait(E, f"T1ev:{p}:{ND - 1}")
                for e in range(NZ):
                    if p > 0 and e == 0:
                        u.wait(E, f"finj:{p - 1}:3:1")
                    if p > 0 and e == 1:
                        u.wait(E, f"fin:{p - 1}:3")
                    if e >= 2:
                        u.wait(E, f"gat:{p}:{e - 2}")
                    for half in range(2):
                        sg = gslab[0]
                        u.wait(E, f"sl:{sg}")
                        pz = pb[2 * (e % 2) + half]
                        for k in range(ND):
                            mm(pz[:, :], wb[sg % NWB][:, k * 128:(k + 1) * 128] if E else None,
                               xnT[:, k * T:(k + 1) * T] if E else None,
                               k == 0, k == ND - 1,
                               inc_label=(f"su:{sg}" if k == ND - 1 else None))
                        gslab[0] += 1
                for P2 in range(2):
                    for e in range(NZ):
                        sg = gslab[0]
                        u.wait(E, f"sl:{sg}")
                        if P2 == 0:
                            u.wait(E, f"gat:{p}:{e}")
                        for tci in range(2):
                            tc = 2 * P2 + tci
                            if e == 0:
                                if P2 == 0 and tci == 1:
                                    u.wait(E, f"gat:{p}:{NZ - 1}")
                            for j in range(4):
                                if e == 0 and P2 == 1:
                                    u.wait(E, f"ev:{p}:{tci}:{j}")
                                mm(pb[(4 if tci == 0 else 0) + j][:, 0:T],
                                   actb[:, e * T + tc * 128: e * T + (tc + 1) * 128] if E else None,
                                   wb[sg % NWB][:, j * T:(j + 1) * T] if E else None,
                                   e == 0, e == NZ - 1,
                                   inc_label=(f"su:{sg}" if (tci == 1 and j == 3) else None))
                        gslab[0] += 1
                u.wait(E, f"ev:{p}:2:1")
                for k in range(ND):
                    if k >= 2:
                        u.wait(E, f"T2evA:{p}:{k - 2}")
                    for tc in range(2):
                        if k == 0:
                            u.wait(E, f"xnf:{p}:{tc}")
                        mm(pb[4 + k % 2][:, tc * 128:(tc + 1) * 128],
                           xn4[:, tc * D + k * 128: tc * D + (k + 1) * 128] if E else None,
                           idmt[:, :] if E else None, True, True,
                           inc_label=(f"T2Amm:{p}:{k}" if tc == 1 else None))
                u.wait(E, f"T2evA:{p}:{ND - 1}")
                for k in range(ND):
                    if k >= 2:
                        u.wait(E, f"T2evB:{p}:{k - 2}")
                    for tc in range(2, 4):
                        if k == 0:
                            u.wait(E, f"xnf:{p}:{tc}")
                        mm(pb[4 + k % 2][:, tc * 128:(tc + 1) * 128],
                           xn4[:, tc * D + k * 128: tc * D + (k + 1) * 128] if E else None,
                           idmt[:, :] if E else None, True, True,
                           inc_label=(f"T2Bmm:{p}:{k}" if tc == 3 else None))
                u.wait(E, f"T2evB:{p}:{ND - 1}")
                for h in range(NH):
                    if h == 0:
                        u.wait(E, f"ev:{p}:3:1")
                    if h == 1:
                        u.wait(E, f"ev:{p}:3:3")
                    if h >= 2:
                        u.wait(E, f"actf:{p}:{h - 2}")
                    for half in range(2):
                        sg = gslab[0]
                        u.wait(E, f"sl:{sg}")
                        pz = pb[2 * (h % 2) + half]
                        for k in range(ND):
                            mm(pz[:, :], wb[sg % NWB][:, k * 128:(k + 1) * 128] if E else None,
                               xnT[:, k * T:(k + 1) * T] if E else None,
                               k == 0, k == ND - 1,
                               inc_label=(f"su:{sg}" if k == ND - 1 else None))
                        gslab[0] += 1
                for P2 in range(2):
                    if P2 == 0:
                        u.wait(E, f"actf:{p}:{NH - 1}")
                    for h in range(NH):
                        sg = gslab[0]
                        u.wait(E, f"sl:{sg}")
                        for tci in range(2):
                            tc = 2 * P2 + tci
                            for j in range(4):
                                if h == 0 and P2 == 1:
                                    u.wait(E, f"fin:{p}:{tci}" if j == 3 else f"finj:{p}:{tci}:{j}")
                                mm(pb[(4 if tci == 0 else 0) + j][:, 0:T],
                                   actb[:, h * T + tc * 128: h * T + (tc + 1) * 128] if E else None,
                                   wb[sg % NWB][:, j * T:(j + 1) * T] if E else None,
                                   h == 0, h == NH - 1,
                                   inc_label=(f"su:{sg}" if (tci == 1 and j == 3) else None))
                        gslab[0] += 1

        def gen_vector2(E):
            def op(fn, label=None):
                if E:
                    E.wait_ge(sems["v"], u.cnt["v"])
                i = fn() if E else None
                u.inc("v", 1, label, i)

            def conv_op(i_):
                u.wait(E, f"qld:{i_}")
                kind = ITEMS[i_][0]
                if p0_direct(kind):
                    g_ = slab_of_item(i_)
                    out_ap = wb[g_ % NWB][:, :]
                    if g_ >= NWB:
                        u.wait(E, f"su:{g_ - NWB}")
                        pk, px = slab_list(0)[g_ - NWB][0], slab_list(0)[g_ - NWB][1]
                        if p0_direct(pk):
                            u.wait(E, f"wqst:{item_of_slab(pk, px)}")
                else:
                    out_ap = qo[QOSLOT[i_]][:, :]
                    if QOPREV[i_] is not None:
                        u.wait(E, f"wqst:{QOPREV[i_]}")
                if kind in ("win", "wout"):
                    op(lambda: E.tensor_copy(out_ap, ring_f32[i_ % 4]), f"wqts:{i_}")
                elif kind in ("wg", "wu"):
                    op(lambda: E.tensor_single_scalar(out_ap, ring_i32[i_ % 4], -128.0, OP.add), f"wqts:{i_}")
                else:
                    op(lambda: E.scalar_tensor_tensor(out_ap, ring_i32[i_ % 4], -128.0,
                                                      sdbt[:, :], OP.add, OP.mult), f"wqts:{i_}")
                if p0_direct(kind):
                    u.alias(f"sl:{slab_of_item(i_)}", f"wqts:{i_}")

            u.wait(E, "dtin")
            op(lambda: E.tensor_reduce(ntmp[0:1, 0:1], dt_row, AX, OP.min), "dtmin")
            u.wait(E, "dtexp")
            op(lambda: E.tensor_reduce(ntmp[0:1, 1:2], dt_exp, AX, OP.add))
            op(lambda: E.reciprocal(ntmp[0:1, 2:3], ntmp[0:1, 1:2]))
            op(lambda: E.tensor_single_scalar(dt_row, dt_exp, ntmp[0:1, 2:3], OP.mult), "dtwrow")

            def ln_stats(p, tc, base):
                for j in range(4):
                    op(lambda p=p, tc=tc, j=j: E.bn_stats(bnb[:, 6 * j: 6 * (j + 1)], xslot_j(p, tc, j)))
                op(lambda tc=tc, base=base: E.bn_aggr(mvb[:, base + 2 * tc: base + 2 * tc + 2], bnb[:, 0:24]))

            def ln1_block(p):
                for tc in range(4):
                    u.wait(E, f"xin:{p}:{tc}")
                    ln_stats(p, tc, 0)
                w = 4
                mu = mvb[:, 0:8:2]
                v1 = mvb[:, 1:8:2]
                C = ntmp[:, 16:16 + w]
                Dv = ntmp[:, 8:8 + w]
                op(lambda: E.tensor_single_scalar(C, v1, EPS, OP.add))
                y1 = newton_rsqrt(E, op, C, w, 7)
                op(lambda y1=y1: E.tensor_single_scalar(coef[:, 8:12], y1, 1.0, OP.add))
                op(lambda y1=y1: E.tensor_tensor(Dv, mu, y1, OP.mult))
                op(lambda: E.tensor_single_scalar(coef[:, 12:16], Dv, -1.0, OP.mult))
                op(lambda: E.reciprocal(Dv, C))
                op(lambda: E.tensor_tensor(Dv, v1, Dv, OP.mult))
                op(lambda y1=y1: E.tensor_copy(coef[:, 0:4], y1))
                op(lambda: E.tensor_single_scalar(C, Dv, EPS, OP.add))
                y2 = newton_rsqrt(E, op, C, w, 4)
                op(lambda y2=y2: E.tensor_tensor(coef[:, 0:4], coef[:, 0:4], y2, OP.mult))
                op(lambda: E.tensor_tensor(Dv, mu, coef[:, 0:4], OP.mult))
                op(lambda: E.tensor_single_scalar(coef[:, 4:8], Dv, -1.0, OP.mult), f"st1:{p}")

            for p in range(NP):
                if p == 0:
                    ln1_block(0)
                    u.wait(E, "cst")
                    for i_ in (0, 1, 3, 4):
                        conv_op(i_)
                for e in range(NZ):
                    sgg = p * NSLAB + 2 * e + 1
                    u.wait(E, f"su:{sgg}")
                    if p == 0 and e == 0:
                        u.wait(E, "dtwcols")
                    b = 4 * (e % 2)
                    pz, pg = pb[2 * (e % 2)], pb[2 * (e % 2) + 1]
                    op(lambda pz=pz, b=b: E.tensor_scalar(gsc[b][:, :], pz[:, :], 15.0, -15.0, OP.min, OP.max), f"zc:{p}:{e}")
                    op(lambda pg=pg, b=b: E.tensor_scalar(gsc[b + 2][:, :], pg[:, :], 15.0, -15.0, OP.min, OP.max), f"gc:{p}:{e}")
                    u.wait(E, f"sg2:{p}:{e}")
                    op(lambda b=b: E.tensor_tensor(gsc[b][:, :], gsc[b + 2][:, :], gsc[b + 3][:, :], OP.mult))
                    op(lambda pz=pz, b=b: E.tensor_tensor(gsc[b + 2][:, :], pz[:, :], gsc[b + 1][:, :], OP.mult))
                    op(lambda b=b, e=e: E.scalar_tensor_tensor(actb[:, e * T:(e + 1) * T], gsc[b + 2][:, :],
                                                               dtwt[:, e:e + 1], gsc[b][:, :], OP.mult, OP.mult), f"gat:{p}:{e}")
                    if p == 0 and e + 2 < NZ:
                        conv_op(3 * (e + 2))
                        conv_op(3 * (e + 2) + 1)
                for P2 in range(2):
                    last_slab = p * NSLAB + 2 * NZ + (P2 + 1) * NZ - 1
                    u.wait(E, f"su:{last_slab}")
                    for tci in range(2):
                        tc = 2 * P2 + tci
                        for j in range(4):
                            u.wait(E, f"ev:{p}:{tc}:{j}")
                            op(lambda p=p, tc=tc, j=j: E.scalar_tensor_tensor(
                                xslot_j(p, tc, j), xslot_j(p, tc, j), coef[:, 8 + tc:9 + tc],
                                gsc[4 * (j % 2)][:, :], OP.mult, OP.add),
                                f"x2:{p}:{tc}" if j == 3 else f"x2j:{p}:{tc}:{j}")
                        ln_stats(p, tc, 8)
                    w = 2
                    mu = mvb[:, 8 + 4 * P2: 8 + 4 * P2 + 3:2]
                    v1 = mvb[:, 9 + 4 * P2: 9 + 4 * P2 + 3:2]
                    C = ntmp[:, 16:16 + w]
                    Dv = ntmp[:, 8:8 + w]
                    op(lambda v1=v1, C=C: E.tensor_single_scalar(C, v1, EPS, OP.add))
                    yf = newton_rsqrt(E, op, C, w, 7)
                    op(lambda yf=yf, P2=P2: E.tensor_single_scalar(coef[:, 16 + 2 * P2:18 + 2 * P2], yf, 1.0, OP.mult))
                    op(lambda mu=mu, yf=yf, Dv=Dv: E.tensor_tensor(Dv, mu, yf, OP.mult))
                    op(lambda Dv=Dv, P2=P2: E.tensor_single_scalar(coef[:, 20 + 2 * P2:22 + 2 * P2], Dv, -1.0, OP.mult), f"st2:{p}:{P2}")
                    if p == 0 and P2 == 1:
                        for k in range(9):
                            conv_op(3 * NZ + k)
                for h in range(NH):
                    sgu = p * NSLAB + 4 * NZ + 2 * h + 1
                    u.wait(E, f"su:{sgu}")
                    u.wait(E, f"sil:{p}:{h}")
                    op(lambda h=h: E.scalar_tensor_tensor(actb[:, h * T:(h + 1) * T],
                                                          pb[2 * (h % 2) + 1][:, :], suct[:, h:h + 1],
                                                          gsc[4 * (h % 2)][:, :], OP.mult, OP.mult), f"actf:{p}:{h}")
                    if p == 0 and h <= 40:
                        conv_op(3 * NZ + 9 + 3 * h + 2)
                if p + 1 < NP:
                    ln1_block(p + 1)
                for P2 in range(2):
                    last_slab = p * NSLAB + 4 * NZ + 2 * NH + (P2 + 1) * NH - 1
                    u.wait(E, f"su:{last_slab}")
                    for tci in range(2):
                        tc = 2 * P2 + tci
                        for j in range(4):
                            if tc > 0:
                                u.wait(E, f"oD:{p}:{tc - 1}:{j}")
                            elif p > 0:
                                u.wait(E, f"oD:{p - 1}:3:{j}")
                            op(lambda p=p, tc=tc, tci=tci, j=j: E.tensor_tensor(
                                gsc[2 * j + 1][:, :], xslot_j(p, tc, j),
                                pb[(4 if tci == 0 else 0) + j][:, :], OP.add),
                                f"fin:{p}:{tc}" if j == 3 else f"finj:{p}:{tc}:{j}")

        def gen_scalar(E):
            def op(fn, label=None):
                if E:
                    E.wait_ge(sems["s"], u.cnt["s"])
                i = fn() if E else None
                u.inc("s", 1, label, i)

            def sconv_op(i_):
                u.wait(E, f"qld:{i_}")
                kind = ITEMS[i_][0]
                if p0_direct(kind):
                    g_ = slab_of_item(i_)
                    out_ap = wb[g_ % NWB][:, :]
                    if g_ >= NWB:
                        u.wait(E, f"su:{g_ - NWB}")
                        pk, px = slab_list(0)[g_ - NWB][0], slab_list(0)[g_ - NWB][1]
                        if p0_direct(pk):
                            u.wait(E, f"wqst:{item_of_slab(pk, px)}")
                    op(lambda: E.activation(out_ap, ring_i32[i_ % 4], AF.Identity,
                                            bias=m128[:, 0:1], scale=1.0), f"wqts:{i_}")
                    u.alias(f"sl:{g_}", f"wqts:{i_}")
                else:
                    out_ap = qo[QOSLOT[i_]][:, :]
                    if QOPREV[i_] is not None:
                        u.wait(E, f"wqst:{QOPREV[i_]}")
                    op(lambda: E.activation(out_ap, ring_f32[i_ % 4], AF.Copy,
                                            bias=0.0, scale=1.0), f"wqts:{i_}")

            u.wait(E, "dtmin")
            op(lambda: E.activation(dt_exp, dt_row, AF.Exp, scale=-1.0, bias=ntmp[0:1, 0:1]), "dtexp")
            for p in range(NP):
                if p == 0:
                    u.wait(E, "st1:0")
                    for tc in range(4):
                        op(lambda tc=tc: E.activation(xn4[:, tc * D:(tc + 1) * D], xslot(0, tc), AF.Identity,
                                                      scale=coef[:, 0 + tc:1 + tc], bias=coef[:, 4 + tc:5 + tc]),
                           f"xn1:0:{tc}")
                for k in range(ND):
                    u.wait(E, f"T1mm:{p}:{k}")
                    if p == 0 and k == 0:
                        u.wait(E, "cst")
                    op(lambda k=k: E.activation(xnT[:, k * T:(k + 1) * T], pb[4 + k % 2][:, :], AF.Identity,
                                                scale=gcct[:, k:k + 1], bias=bcct[:, k:k + 1]), f"T1ev:{p}:{k}")
                for e in range(NZ):
                    if e == 0 and p > 0:
                        for jj in range(4):
                            u.wait(E, f"oD:{p - 1}:3:{jj}")
                    if e >= 2:
                        u.wait(E, f"gat:{p}:{e - 2}")
                    b = 4 * (e % 2)
                    u.wait(E, f"zc:{p}:{e}")
                    op(lambda b=b: E.activation(gsc[b + 1][:, :], gsc[b][:, :], AF.Sigmoid), f"sz:{p}:{e}")
                    u.wait(E, f"gc:{p}:{e}")
                    op(lambda b=b: E.activation(gsc[b + 3][:, :], gsc[b + 2][:, :], AF.Sigmoid), f"sg2:{p}:{e}")
                    if p == 0 and e == 0:
                        sconv_op(2)
                        sconv_op(5)
                    if p == 0 and e + 2 < NZ:
                        sconv_op(3 * (e + 2) + 2)
                seq = [(2 * P2s + tci, j) for P2s in range(2) for tci in range(2) for j in range(4)]
                for k, (tcs, j) in enumerate(seq):
                    if j == 0 and tcs % 2 == 0:
                        u.wait(E, f"su:{k2g_m2(p, tcs // 2)}")
                    if k >= 2:
                        ptc, pj = seq[k - 2]
                        u.wait(E, f"x2:{p}:{ptc}" if pj == 3 else f"x2j:{p}:{ptc}:{pj}")
                    tci = tcs % 2
                    op(lambda p=p, tcs=tcs, tci=tci, j=j: E.activation(
                        gsc[4 * (j % 2)][:, :], pb[(4 if tci == 0 else 0) + j][:, :], AF.Identity,
                        scale=1.0, bias=coef[:, 12 + tcs:13 + tcs]), f"ev:{p}:{tcs}:{j}")
                u.wait(E, f"st2:{p}:0")
                for tc in range(2):
                    op(lambda p=p, tc=tc: E.activation(xn4[:, tc * D:(tc + 1) * D], xslot(p, tc), AF.Identity,
                                                       scale=coef[:, 16 + tc:17 + tc], bias=coef[:, 20 + tc:21 + tc]),
                       f"xnf:{p}:{tc}")
                for k in range(ND):
                    u.wait(E, f"T2Amm:{p}:{k}")
                    op(lambda k=k: E.activation(xnT[:, k * T:k * T + 256], pb[4 + k % 2][:, 0:256], AF.Identity),
                       f"T2evA:{p}:{k}")
                u.wait(E, f"st2:{p}:1")
                for tc in range(2, 4):
                    op(lambda p=p, tc=tc: E.activation(xn4[:, tc * D:(tc + 1) * D], xslot(p, tc), AF.Identity,
                                                       scale=coef[:, 16 + tc:17 + tc], bias=coef[:, 20 + tc:21 + tc]),
                       f"xnf:{p}:{tc}")
                for k in range(ND):
                    u.wait(E, f"T2Bmm:{p}:{k}")
                    op(lambda k=k: E.activation(xnT[:, k * T + 256:(k + 1) * T], pb[4 + k % 2][:, 256:512], AF.Identity),
                       f"T2evB:{p}:{k}")
                for h in range(NH):
                    if h == 0:
                        u.wait(E, f"gat:{p}:{NZ - 1}")
                    if h >= 2:
                        u.wait(E, f"actf:{p}:{h - 2}")
                    sgg = p * NSLAB + 4 * NZ + 2 * h
                    u.wait(E, f"su:{sgg}")
                    op(lambda h=h: E.activation(gsc[4 * (h % 2)][:, :], pb[2 * (h % 2)][:, :], AF.Silu,
                                                scale=sgct[:, h:h + 1]), f"sil:{p}:{h}")
                    if p == 0 and h <= 40:
                        sconv_op(3 * NZ + 9 + 3 * h)
                        sconv_op(3 * NZ + 9 + 3 * h + 1)
                if p + 1 < NP:
                    u.wait(E, f"st1:{p + 1}")
                    for tc in range(4):
                        op(lambda p=p, tc=tc: E.activation(xn4[:, tc * D:(tc + 1) * D], xslot(p + 1, tc), AF.Identity,
                                                           scale=coef[:, 0 + tc:1 + tc], bias=coef[:, 4 + tc:5 + tc]),
                           f"xn1:{p + 1}:{tc}")

        gens = dict(gpsimd=gen_gpsimd, sync=gen_sync, tensor=gen_tensor,
                    vector=gen_vector2, scalar=gen_scalar)
        for name, g in gens.items():
            g(None)
        u.start_emit(sems)

        with nc.Block() as blk:
            @blk.gpsimd
            def _(e):
                for s in sems.values():
                    e.sem_clear(s)
                e.memset(m128[:, :], -128.0)

        with nc.Block() as blk:
            @blk.sync
            def _(e):
                gens["sync"](e)

            @blk.tensor
            def _(e):
                gens["tensor"](e)

            @blk.vector
            def _(e):
                gens["vector"](e)

            @blk.scalar
            def _(e):
                gens["scalar"](e)

            @blk.gpsimd
            def _(e):
                gens["gpsimd"](e)

    return nc


_BF = ml_dtypes.bfloat16


def _host_prep(inputs):
    x = np.ascontiguousarray(np.asarray(inputs["L0_attn_x"], dtype=np.float32).reshape(NCORES * TPC, D))
    Win = np.asarray(inputs["L0_attn_W_in"], dtype=np.float32)
    Wout = np.asarray(inputs["L0_attn_W_out"], dtype=np.float32)
    gamma = np.asarray(inputs["L0_attn_gamma"], dtype=np.float32)
    beta = np.asarray(inputs["L0_attn_beta"], dtype=np.float32)
    dt = np.asarray(inputs["L0_attn_dt"], dtype=np.float32)
    wq_g = np.asarray(inputs["L0_ffn_wq_gate"], dtype=np.int32)
    wq_u = np.asarray(inputs["L0_ffn_wq_up"], dtype=np.int32)
    wq_d = np.asarray(inputs["L0_ffn_wq_down"], dtype=np.int32)
    s_g = np.asarray(inputs["L0_ffn_s_gate"], dtype=np.float32)
    s_u = np.asarray(inputs["L0_ffn_s_up"], dtype=np.float32)
    s_d = np.asarray(inputs["L0_ffn_s_down"], dtype=np.float32)

    def rowslab(WT, nchunk):
        return np.ascontiguousarray(WT.reshape(ND, 128, nchunk, 128).transpose(2, 1, 0, 3).reshape(nchunk, 128, D))

    winT_h = np.ascontiguousarray(rowslab(Win.T, NE2)[ZIG])
    woutT_h = np.ascontiguousarray(Wout.T)
    wqg_h = rowslab(wq_g.T, NH)
    wqu_h = rowslab(wq_u.T, NH)
    wqd_h = np.ascontiguousarray(wq_d.T)
    sgc_h = np.ascontiguousarray(s_g.reshape(NH, 128).T)
    suc_h = np.ascontiguousarray(s_u.reshape(NH, 128).T)
    gcc_h = np.ascontiguousarray(gamma.reshape(ND, 128).T)
    bcc_h = np.ascontiguousarray(beta.reshape(ND, 128).T)
    sdb_h = np.ascontiguousarray(np.broadcast_to(s_d, (128, D)))
    dtr_h = np.ascontiguousarray(dt.reshape(1, 4096))
    idm_h = np.eye(128, dtype=_BF)

    shared = dict(winT=winT_h, woutT=woutT_h, wqg=wqg_h, wqu=wqu_h, wqd=wqd_h,
                  sgc=sgc_h, suc=suc_h, gcc=gcc_h, bcc=bcc_h, sdb=sdb_h,
                  dtr=dtr_h, idm=idm_h)
    in_maps = []
    for c in range(NCORES):
        m = dict(shared)
        m["x"] = np.ascontiguousarray(x[c * TPC:(c + 1) * TPC])
        in_maps.append(m)
    return in_maps


def kernel(**inputs):
    in_maps = _host_prep(inputs)
    nc = build_program()
    res = run_bass_kernel_spmd(nc, in_maps, core_ids=list(range(NCORES)))
    out = np.concatenate([np.asarray(r["out"], dtype=np.float32) for r in res.results], axis=0)
    return out.reshape(4, 8192, D)


# version 14
# speedup vs baseline: 1.1824x; 1.0250x over previous
import numpy as np
import ml_dtypes
import concourse.bass as bass
import concourse.mybir as mybir
from concourse.bass_utils import run_bass_kernel_spmd
from concourse.alu_op_type import AluOpType as OP

F32 = mybir.dt.float32
BF16 = mybir.dt.bfloat16
I32 = mybir.dt.int32
U8 = mybir.dt.uint8
AF = mybir.ActivationFunctionType
AX = mybir.AxisListType.X

NCORES = 8
TPC = 4096
T = 512
NP = TPC // T
D = 2048
ND = 16
NZ = 32
NE2 = 64
H = 5632
NH = 44
EPS = 1e-5
NWB = 8

ZIG = np.empty(NE2, np.int64)
ZIG[0::2] = np.arange(NZ)
ZIG[1::2] = np.arange(NZ, NE2)


class U:

    def __init__(self):
        import collections
        self.cnt = collections.defaultdict(int)
        self.lab = {}
        self.emit = False
        self.sems = None

    def start_emit(self, sems):
        import collections
        self.emit = True
        self.sems = sems
        self.cnt = collections.defaultdict(int)

    def inc(self, sem, n, label=None, instr=None):
        self.cnt[sem] += n
        if label is not None:
            if not self.emit:
                self.lab[label] = (sem, self.cnt[sem])
            else:
                assert self.lab[label] == (sem, self.cnt[sem]), (label, self.lab[label], sem, self.cnt[sem])
        if instr is not None:
            instr.then_inc(self.sems[sem], n)

    def wait(self, E, label):
        if E is None:
            return
        sem, val = self.lab[label]
        E.wait_ge(self.sems[sem], val)

    def alias(self, name, existing):
        if not self.emit:
            self.lab[name] = self.lab[existing]
        else:
            assert self.lab[name] == self.lab[existing]


def build_program():
    nc = bass.Bass("TRN2", target_bir_lowering=False)

    x_in = nc.dram_tensor("x", [TPC, D], F32, kind="ExternalInput")
    out_hbm = nc.dram_tensor("out", [TPC, D], F32, kind="ExternalOutput")
    winT = nc.dram_tensor("winT", [NE2, 128, D], F32, kind="ExternalInput")
    woutT = nc.dram_tensor("woutT", [4096, D], F32, kind="ExternalInput")
    wqg = nc.dram_tensor("wqg", [NH, 128, D], U8, kind="ExternalInput")
    wqu = nc.dram_tensor("wqu", [NH, 128, D], U8, kind="ExternalInput")
    wqd = nc.dram_tensor("wqd", [H, D], U8, kind="ExternalInput")
    sgc_in = nc.dram_tensor("sgc", [128, NH], F32, kind="ExternalInput")
    suc_in = nc.dram_tensor("suc", [128, NH], F32, kind="ExternalInput")
    gcc_in = nc.dram_tensor("gcc", [128, ND], F32, kind="ExternalInput")
    bcc_in = nc.dram_tensor("bcc", [128, ND], F32, kind="ExternalInput")
    sdb_in = nc.dram_tensor("sdb", [128, D], F32, kind="ExternalInput")
    dtr_in = nc.dram_tensor("dtr", [1, 4096], F32, kind="ExternalInput")
    idm_in = nc.dram_tensor("idm", [128, 128], BF16, kind="ExternalInput")

    win_s = nc.dram_tensor("win_s", [NE2, 128, D], BF16, kind="Internal")
    wout_s = nc.dram_tensor("wout_s", [4096, D], BF16, kind="Internal")
    wg_s = nc.dram_tensor("wg_s", [NH, 128, D], BF16, kind="Internal")
    wu_s = nc.dram_tensor("wu_s", [NH, 128, D], BF16, kind="Internal")
    wd_s = nc.dram_tensor("wd_s", [H, D], BF16, kind="Internal")
    dtw_s = nc.dram_tensor("dtw_s", [1, 4096], F32, kind="Internal")

    u = U()

    import contextlib
    with contextlib.ExitStack() as st:
        def sb(name, shape, dt):
            return st.enter_context(nc.sbuf_tensor(name, shape, dt))

        xres = sb("xres", [128, 4 * D], F32)
        xb2 = sb("xb2", [128, D], F32)
        xb3 = sb("xb3", [128, D], F32)
        xn4 = sb("xn4", [128, 4 * D], BF16)
        xnT = sb("xnT", [128, ND * T], BF16)
        actb = sb("actb", [128, NH * T], BF16)
        wb = [sb(f"wb{i}", [128, D], BF16) for i in range(NWB)]
        gsc = [sb(f"gsc{i}", [128, T], F32) for i in range(8)]
        qi = [sb(f"qi{i}", [128, D], I32) for i in range(2)]
        qo = [sb(f"qo{i}", [128, D], BF16) for i in range(2)]
        sdbt = sb("sdbt", [128, D], F32)
        sgct = sb("sgct", [128, NH], F32)
        suct = sb("suct", [128, NH], F32)
        gcct = sb("gcct", [128, ND], F32)
        bcct = sb("bcct", [128, ND], F32)
        dtwt = sb("dtwt", [128, NZ], F32)
        idmt = sb("idmt", [128, 128], BF16)
        bnb = sb("bnb", [128, 24], F32)
        m128 = sb("m128", [128, 1], F32)
        mvb = sb("mvb", [128, 16], F32)
        coef = sb("coef", [128, 24], F32)
        ntmp = sb("ntmp", [128, 24], F32)

        pb = [st.enter_context(nc.psum_tensor(f"pb{i}", [128, T], F32)) for i in range(8)]

        sem_names = (["t", "v", "s", "g", "cst", "dt1", "dt2", "dt3", "od0", "od1", "od2", "od3"]
                     + [f"wb{i}" for i in range(NWB)]
                     + [f"qi{i}" for i in range(4)]
                     + [f"qs{i}" for i in range(4)]
                     + [f"x{i}" for i in range(8)])
        sems = {n: st.enter_context(nc.semaphore("s_" + n)) for n in sem_names}

        qi_f32 = [qi[0][:, :].bitcast(F32), qi[1][:, :].bitcast(F32)]
        ring_i32 = [qi[0][:, :], qi[1][:, :], xb2[:, :].bitcast(I32), xb3[:, :].bitcast(I32)]
        ring_f32 = [qi_f32[0], qi_f32[1], xb2[:, :], xb3[:, :]]
        ring_u8 = [r.bitcast(U8)[:, 0:D] for r in ring_f32]

        def xslot(p, tc):
            if p % 2 == 0:
                return xres[:, tc * D:(tc + 1) * D]
            return [xb2[:, :], xb3[:, :], qi_f32[0], qi_f32[1]][tc]

        def xslot_j(p, tc, j):
            s = xslot(p, tc)
            if p % 2 == 0:
                return xres[:, tc * D + j * T: tc * D + (j + 1) * T]
            return [xb2, xb3, None, None][tc][:, j * T:(j + 1) * T] if tc < 2 else qi_f32[tc - 2][:, j * T:(j + 1) * T]

        dt_row = actb[0:1, 0:8192].bitcast(F32)
        dt_exp = actb[0:1, 8192:16384].bitcast(F32)

        def slab_list(p):
            out = []
            for e in range(NZ):
                out.append(("win", 2 * e))
                out.append(("win", 2 * e + 1))
            for P2 in range(2):
                for e in range(NZ):
                    out.append(("wout", e, P2))
            for h in range(NH):
                out.append(("wg", h))
                out.append(("wu", h))
            for P2 in range(2):
                for h in range(NH):
                    out.append(("wd", h, P2))
            return out

        NSLAB = len(slab_list(0))

        def k2g_m2(p, P2):
            return p * NSLAB + 2 * NZ + (P2 + 1) * NZ - 1

        ITEMS = []
        for e in range(NZ):
            ITEMS.append(("win", 2 * e))
            ITEMS.append(("win", 2 * e + 1))
            ITEMS.append(("wout", e))
        for h in range(NH):
            ITEMS.append(("wg", h))
            ITEMS.append(("wu", h))
            ITEMS.append(("wd", h))
        NIT = len(ITEMS)

        def item_of_slab(kind, idx):
            if kind == "win":
                return 3 * (idx // 2) + (idx % 2)
            if kind == "wout":
                return 3 * idx + 2
            base = 3 * NZ
            off = {"wg": 0, "wu": 1, "wd": 2}[kind]
            return base + 3 * idx + off

        def item_src(i_):
            kind, idx = ITEMS[i_]
            if kind == "win":
                return winT[idx]
            if kind == "wout":
                return woutT[idx * 128:(idx + 1) * 128, :]
            if kind == "wg":
                return wqg[idx]
            if kind == "wu":
                return wqu[idx]
            return wqd[idx * 128:(idx + 1) * 128, :]

        def p0_direct(kind):
            return kind in ("win", "wg", "wu")

        def slab_of_item(i_):
            kind, idx = ITEMS[i_]
            if kind == "win":
                return idx
            if kind == "wout":
                return 2 * NZ + idx
            if kind == "wg":
                return 4 * NZ + 2 * idx
            if kind == "wu":
                return 4 * NZ + 2 * idx + 1
            return 4 * NZ + 2 * NH + idx

        QOSLOT = {}
        _qct = 0
        for _i, (_k, _x) in enumerate(ITEMS):
            if not p0_direct(_k):
                QOSLOT[_i] = _qct % 2
                _qct += 1

        QOPREV = {}
        _last = {0: None, 1: None}
        for _i in sorted(QOSLOT):
            QOPREV[_i] = _last[QOSLOT[_i]]
            _last[QOSLOT[_i]] = _i

        def item_dst(i_):
            kind, idx = ITEMS[i_]
            if kind == "win":
                return win_s[idx]
            if kind == "wout":
                return wout_s[idx * 128:(idx + 1) * 128, :]
            if kind == "wg":
                return wg_s[idx]
            if kind == "wu":
                return wu_s[idx]
            return wd_s[idx * 128:(idx + 1) * 128, :]

        def newton_rsqrt(E, op, vp_ap, w, iters):
            A = ntmp[:, 0:w]
            B = ntmp[:, 4:4 + w]
            Dv = ntmp[:, 8:8 + w]
            Ev = ntmp[:, 12:12 + w]
            op(lambda: E.tensor_single_scalar(Dv, vp_ap, 0.4, OP.max))
            op(lambda: E.reciprocal(A, Dv))
            y, yn = A, B
            for _ in range(iters):
                op(lambda y=y: E.tensor_tensor(Dv, y, y, OP.mult))
                op(lambda: E.tensor_tensor(Ev, Dv, vp_ap, OP.mult))
                op(lambda: E.tensor_scalar(Ev, Ev, -0.5, 1.5, OP.mult, OP.add))
                op(lambda y=y, yn=yn: E.tensor_tensor(yn, y, Ev, OP.mult))
                y, yn = yn, y
            return y

        def wq_conv_sem(i_):
            return "s" if NH <= i_ < 2 * NH else "v"

        def gen_gpsimd(E):
            def dma(sem, out, in_, label=None, **kw):
                i = E.dma_start(out, in_, **kw) if E else None
                u.inc(sem, 16, label, i)

            for tc in range(4):
                dma(f"x{tc}", xslot(0, tc), x_in[tc * 128:(tc + 1) * 128, :], label=f"xin:0:{tc}")
            dma("cst", sgct[:, :], sgc_in[:, :])
            dma("cst", suct[:, :], suc_in[:, :])
            dma("cst", gcct[:, :], gcc_in[:, :])
            dma("cst", bcct[:, :], bcc_in[:, :])
            dma("cst", sdbt[:, :], sdb_in[:, :])
            dma("cst", idmt[:, :], idm_in[:, :], label="cst")
            dma("dt1", dt_row, dtr_in[:, :], label="dtin")
            def qld(i_):
                kind = ITEMS[i_][0]
                dst = ring_f32[i_ % 4] if kind in ("win", "wout") else ring_u8[i_ % 4]
                ii = E.dma_start(dst, item_src(i_)) if E else None
                u.inc(f"qi{i_ % 4}", 16, f"qld:{i_}", ii)

            for i_ in range(4):
                qld(i_)
            u.wait(E, "dtwrow")
            dma("dt2", dtw_s[:, :], dt_row, label="dtwst")
            u.wait(E, "dtwst")
            in_ap = dtw_s[:, :].rearrange("a (j p) -> p (a j)", p=128) if E else None
            i2 = E.dma_start(dtwt[:, 0:NZ], in_ap, allow_slow_non_contiguous=True) if E else None
            u.inc("dt3", 16, "dtwcols", i2)
            for i_ in range(NIT):
                if i_ >= 1:
                    u.wait(E, f"wqst:{i_ - 1}")
                    if E:
                        E.sem_inc(sems["g"], 1)
                    u.inc("g", 1, f"cf:{i_ - 1}", None)
                u.wait(E, f"wqts:{i_}")
                if p0_direct(ITEMS[i_][0]):
                    st_src = wb[slab_of_item(i_) % NWB][:, :]
                else:
                    st_src = qo[QOSLOT[i_]][:, :]
                ii = E.dma_start(item_dst(i_), st_src) if E else None
                u.inc(f"qs{i_ % 4}", 16, f"wqst:{i_}", ii)
                if i_ + 4 < NIT:
                    qld(i_ + 4)
            u.wait(E, f"wqst:{NIT - 1}")
            if E:
                E.sem_inc(sems["g"], 1)
            u.inc("g", 1, f"cf:{NIT - 1}", None)
            u.wait(E, f"wqts:{NIT - 2}")
            dma("x4", xslot(1, 0), x_in[T:T + 128, :], label="xin:1:0")
            u.wait(E, f"wqts:{NIT - 1}")
            dma("x5", xslot(1, 1), x_in[T + 128:T + 256, :], label="xin:1:1")
            u.wait(E, f"wqts:{NIT - 4}")
            dma("x6", xslot(1, 2), x_in[T + 256:T + 384, :], label="xin:1:2")
            u.wait(E, f"wqts:{NIT - 3}")
            dma("x7", xslot(1, 3), x_in[T + 384:T + 512, :], label="xin:1:3")

            for p in range(NP):
                for tc in range(4):
                    r0 = p * T + tc * 128
                    for j in range(4):
                        u.wait(E, f"fin:{p}:{tc}" if j == 3 else f"finj:{p}:{tc}:{j}")
                        ii = E.dma_start(out_hbm[r0:r0 + 128, j * T:(j + 1) * T],
                                         gsc[2 * j + 1][:, :]) if E else None
                        u.inc(f"od{j}", 16, f"oD:{p}:{tc}:{j}", ii)
                    if p + 2 < NP:
                        r1 = (p + 2) * T + tc * 128
                        ii = E.dma_start(xslot(p + 2, tc), x_in[r1:r1 + 128, :]) if E else None
                        u.inc(f"x{4 * (p % 2) + tc}", 16, f"xin:{p + 2}:{tc}", ii)
            if E:
                for jj in range(4):
                    E.wait_ge(sems[f"od{jj}"], u.cnt[f"od{jj}"])

        def gen_sync(E):
            g = 0
            NS0 = len(slab_list(0))
            for p in range(NP):
                for item in slab_list(p):
                    kind = item[0]
                    if p == 0 and p0_direct(kind):
                        g += 1
                        continue
                    if g >= NWB:
                        u.wait(E, f"su:{g - NWB}")
                        gprev = g - NWB
                        if gprev < NS0:
                            pk = slab_list(0)[gprev][0]
                            if p0_direct(pk):
                                u.wait(E, f"wqst:{item_of_slab(pk, slab_list(0)[gprev][1])}")
                    if p <= 1:
                        u.wait(E, f"cf:{item_of_slab(kind, item[1])}")
                    ii = None
                    if E:
                        slot = wb[g % NWB]
                        if kind == "win":
                            src = win_s[item[1]]
                        elif kind == "wout":
                            src = wout_s[item[1] * 128:(item[1] + 1) * 128, :]
                        elif kind == "wg":
                            src = wg_s[item[1]]
                        elif kind == "wu":
                            src = wu_s[item[1]]
                        else:
                            src = wd_s[item[1] * 128:(item[1] + 1) * 128, :]
                        ii = E.dma_start(slot[:, :], src)
                    u.inc(f"wb{g % NWB}", 16, f"sl:{g}", ii)
                    g += 1

        def gen_tensor(E):
            gslab = [0]

            def mm(out_ap, lhsT, rhs, start, stop, inc_label=None):
                i = E.matmul(out_ap, lhsT, rhs, start=start, stop=stop) if E else None
                if inc_label is not None:
                    u.inc("t", 1, inc_label, i)

            for p in range(NP):
                if p > 0:
                    u.wait(E, f"finj:{p - 1}:2:1")
                else:
                    u.wait(E, "cst")
                for k in range(ND):
                    if k >= 2:
                        u.wait(E, f"T1ev:{p}:{k - 2}")
                    for tc in range(4):
                        if k == 0:
                            u.wait(E, f"xn1:{p}:{tc}")
                        mm(pb[4 + k % 2][:, tc * 128:(tc + 1) * 128],
                           xn4[:, tc * D + k * 128: tc * D + (k + 1) * 128] if E else None,
                           idmt[:, :] if E else None, True, True,
                           inc_label=(f"T1mm:{p}:{k}" if tc == 3 else None))
                u.wait(E, f"T1ev:{p}:{ND - 1}")
                for e in range(NZ):
                    if p > 0 and e == 0:
                        u.wait(E, f"finj:{p - 1}:3:1")
                    if p > 0 and e == 1:
                        u.wait(E, f"fin:{p - 1}:3")
                    if e >= 2:
                        u.wait(E, f"gat:{p}:{e - 2}")
                    for half in range(2):
                        sg = gslab[0]
                        u.wait(E, f"sl:{sg}")
                        pz = pb[2 * (e % 2) + half]
                        for k in range(ND):
                            mm(pz[:, :], wb[sg % NWB][:, k * 128:(k + 1) * 128] if E else None,
                               xnT[:, k * T:(k + 1) * T] if E else None,
                               k == 0, k == ND - 1,
                               inc_label=(f"su:{sg}" if k == ND - 1 else None))
                        gslab[0] += 1
                for P2 in range(2):
                    for e in range(NZ):
                        sg = gslab[0]
                        u.wait(E, f"sl:{sg}")
                        if P2 == 0:
                            u.wait(E, f"gat:{p}:{e}")
                        for tci in range(2):
                            tc = 2 * P2 + tci
                            if e == 0:
                                if P2 == 0 and tci == 1:
                                    u.wait(E, f"gat:{p}:{NZ - 1}")
                            for j in range(4):
                                if e == 0 and P2 == 1:
                                    u.wait(E, f"ev:{p}:{tci}:{j}")
                                mm(pb[(4 if tci == 0 else 0) + j][:, 0:T],
                                   actb[:, e * T + tc * 128: e * T + (tc + 1) * 128] if E else None,
                                   wb[sg % NWB][:, j * T:(j + 1) * T] if E else None,
                                   e == 0, e == NZ - 1,
                                   inc_label=(f"su:{sg}" if (tci == 1 and j == 3) else None))
                        gslab[0] += 1
                u.wait(E, f"ev:{p}:2:1")
                for k in range(ND):
                    if k >= 2:
                        u.wait(E, f"T2evA:{p}:{k - 2}")
                    for tc in range(2):
                        if k == 0:
                            u.wait(E, f"xnf:{p}:{tc}")
                        mm(pb[4 + k % 2][:, tc * 128:(tc + 1) * 128],
                           xn4[:, tc * D + k * 128: tc * D + (k + 1) * 128] if E else None,
                           idmt[:, :] if E else None, True, True,
                           inc_label=(f"T2Amm:{p}:{k}" if tc == 1 else None))
                u.wait(E, f"T2evA:{p}:{ND - 1}")
                for k in range(ND):
                    if k >= 2:
                        u.wait(E, f"T2evB:{p}:{k - 2}")
                    for tc in range(2, 4):
                        if k == 0:
                            u.wait(E, f"xnf:{p}:{tc}")
                        mm(pb[4 + k % 2][:, tc * 128:(tc + 1) * 128],
                           xn4[:, tc * D + k * 128: tc * D + (k + 1) * 128] if E else None,
                           idmt[:, :] if E else None, True, True,
                           inc_label=(f"T2Bmm:{p}:{k}" if tc == 3 else None))
                u.wait(E, f"T2evB:{p}:{ND - 1}")
                for h in range(NH):
                    if h == 0:
                        u.wait(E, f"ev:{p}:3:1")
                    if h == 1:
                        u.wait(E, f"ev:{p}:3:3")
                    if h >= 2:
                        u.wait(E, f"actf:{p}:{h - 2}")
                    for half in range(2):
                        sg = gslab[0]
                        u.wait(E, f"sl:{sg}")
                        pz = pb[2 * (h % 2) + half]
                        for k in range(ND):
                            mm(pz[:, :], wb[sg % NWB][:, k * 128:(k + 1) * 128] if E else None,
                               xnT[:, k * T:(k + 1) * T] if E else None,
                               k == 0, k == ND - 1,
                               inc_label=(f"su:{sg}" if k == ND - 1 else None))
                        gslab[0] += 1
                for P2 in range(2):
                    if P2 == 0:
                        u.wait(E, f"actf:{p}:{NH - 1}")
                    for h in range(NH):
                        sg = gslab[0]
                        u.wait(E, f"sl:{sg}")
                        for tci in range(2):
                            tc = 2 * P2 + tci
                            for j in range(4):
                                if h == 0 and P2 == 1:
                                    u.wait(E, f"fin:{p}:{tci}" if j == 3 else f"finj:{p}:{tci}:{j}")
                                mm(pb[(4 if tci == 0 else 0) + j][:, 0:T],
                                   actb[:, h * T + tc * 128: h * T + (tc + 1) * 128] if E else None,
                                   wb[sg % NWB][:, j * T:(j + 1) * T] if E else None,
                                   h == 0, h == NH - 1,
                                   inc_label=(f"su:{sg}" if (tci == 1 and j == 3) else None))
                        gslab[0] += 1

        def gen_vector2(E):
            def op(fn, label=None):
                if E:
                    E.wait_ge(sems["v"], u.cnt["v"])
                i = fn() if E else None
                u.inc("v", 1, label, i)

            def conv_op(i_):
                u.wait(E, f"qld:{i_}")
                kind = ITEMS[i_][0]
                if p0_direct(kind):
                    g_ = slab_of_item(i_)
                    out_ap = wb[g_ % NWB][:, :]
                    if g_ >= NWB:
                        u.wait(E, f"su:{g_ - NWB}")
                        pk, px = slab_list(0)[g_ - NWB][0], slab_list(0)[g_ - NWB][1]
                        if p0_direct(pk):
                            u.wait(E, f"wqst:{item_of_slab(pk, px)}")
                else:
                    out_ap = qo[QOSLOT[i_]][:, :]
                    if QOPREV[i_] is not None:
                        u.wait(E, f"wqst:{QOPREV[i_]}")
                if kind in ("win", "wout"):
                    op(lambda: E.tensor_copy(out_ap, ring_f32[i_ % 4]), f"wqts:{i_}")
                elif kind in ("wg", "wu"):
                    op(lambda: E.tensor_single_scalar(out_ap, ring_u8[i_ % 4], -128.0, OP.add), f"wqts:{i_}")
                else:
                    op(lambda: E.scalar_tensor_tensor(out_ap, ring_u8[i_ % 4], -128.0,
                                                      sdbt[:, :], OP.add, OP.mult), f"wqts:{i_}")
                if p0_direct(kind):
                    u.alias(f"sl:{slab_of_item(i_)}", f"wqts:{i_}")

            u.wait(E, "dtin")
            op(lambda: E.tensor_reduce(ntmp[0:1, 0:1], dt_row, AX, OP.min), "dtmin")
            u.wait(E, "dtexp")
            op(lambda: E.tensor_reduce(ntmp[0:1, 1:2], dt_exp, AX, OP.add))
            op(lambda: E.reciprocal(ntmp[0:1, 2:3], ntmp[0:1, 1:2]))
            op(lambda: E.tensor_single_scalar(dt_row, dt_exp, ntmp[0:1, 2:3], OP.mult), "dtwrow")

            def ln_stats(p, tc, base):
                for j in range(4):
                    op(lambda p=p, tc=tc, j=j: E.bn_stats(bnb[:, 6 * j: 6 * (j + 1)], xslot_j(p, tc, j)))
                op(lambda tc=tc, base=base: E.bn_aggr(mvb[:, base + 2 * tc: base + 2 * tc + 2], bnb[:, 0:24]))

            def ln1_block(p):
                for tc in range(4):
                    u.wait(E, f"xin:{p}:{tc}")
                    ln_stats(p, tc, 0)
                w = 4
                mu = mvb[:, 0:8:2]
                v1 = mvb[:, 1:8:2]
                C = ntmp[:, 16:16 + w]
                Dv = ntmp[:, 8:8 + w]
                op(lambda: E.tensor_single_scalar(C, v1, EPS, OP.add))
                y1 = newton_rsqrt(E, op, C, w, 7)
                op(lambda y1=y1: E.tensor_single_scalar(coef[:, 8:12], y1, 1.0, OP.add))
                op(lambda y1=y1: E.tensor_tensor(Dv, mu, y1, OP.mult))
                op(lambda: E.tensor_single_scalar(coef[:, 12:16], Dv, -1.0, OP.mult))
                op(lambda: E.reciprocal(Dv, C))
                op(lambda: E.tensor_tensor(Dv, v1, Dv, OP.mult))
                op(lambda y1=y1: E.tensor_copy(coef[:, 0:4], y1))
                op(lambda: E.tensor_single_scalar(C, Dv, EPS, OP.add))
                y2 = newton_rsqrt(E, op, C, w, 4)
                op(lambda y2=y2: E.tensor_tensor(coef[:, 0:4], coef[:, 0:4], y2, OP.mult))
                op(lambda: E.tensor_tensor(Dv, mu, coef[:, 0:4], OP.mult))
                op(lambda: E.tensor_single_scalar(coef[:, 4:8], Dv, -1.0, OP.mult), f"st1:{p}")

            for p in range(NP):
                if p == 0:
                    ln1_block(0)
                    u.wait(E, "cst")
                    for i_ in (0, 1, 3, 4):
                        conv_op(i_)
                for e in range(NZ):
                    sgg = p * NSLAB + 2 * e + 1
                    u.wait(E, f"su:{sgg}")
                    if p == 0 and e == 0:
                        u.wait(E, "dtwcols")
                    b = 4 * (e % 2)
                    pz, pg = pb[2 * (e % 2)], pb[2 * (e % 2) + 1]
                    op(lambda pz=pz, b=b: E.tensor_scalar(gsc[b][:, :], pz[:, :], 15.0, -15.0, OP.min, OP.max), f"zc:{p}:{e}")
                    op(lambda pg=pg, b=b: E.tensor_scalar(gsc[b + 2][:, :], pg[:, :], 15.0, -15.0, OP.min, OP.max), f"gc:{p}:{e}")
                    u.wait(E, f"sg2:{p}:{e}")
                    op(lambda b=b: E.tensor_tensor(gsc[b][:, :], gsc[b + 2][:, :], gsc[b + 3][:, :], OP.mult))
                    op(lambda pz=pz, b=b: E.tensor_tensor(gsc[b + 2][:, :], pz[:, :], gsc[b + 1][:, :], OP.mult))
                    op(lambda b=b, e=e: E.scalar_tensor_tensor(actb[:, e * T:(e + 1) * T], gsc[b + 2][:, :],
                                                               dtwt[:, e:e + 1], gsc[b][:, :], OP.mult, OP.mult), f"gat:{p}:{e}")
                    if p == 0 and e + 2 < NZ:
                        conv_op(3 * (e + 2))
                        conv_op(3 * (e + 2) + 1)
                for P2 in range(2):
                    last_slab = p * NSLAB + 2 * NZ + (P2 + 1) * NZ - 1
                    u.wait(E, f"su:{last_slab}")
                    for tci in range(2):
                        tc = 2 * P2 + tci
                        for j in range(4):
                            u.wait(E, f"ev:{p}:{tc}:{j}")
                            op(lambda p=p, tc=tc, j=j: E.scalar_tensor_tensor(
                                xslot_j(p, tc, j), xslot_j(p, tc, j), coef[:, 8 + tc:9 + tc],
                                gsc[4 * (j % 2)][:, :], OP.mult, OP.add),
                                f"x2:{p}:{tc}" if j == 3 else f"x2j:{p}:{tc}:{j}")
                        ln_stats(p, tc, 8)
                    w = 2
                    mu = mvb[:, 8 + 4 * P2: 8 + 4 * P2 + 3:2]
                    v1 = mvb[:, 9 + 4 * P2: 9 + 4 * P2 + 3:2]
                    C = ntmp[:, 16:16 + w]
                    Dv = ntmp[:, 8:8 + w]
                    op(lambda v1=v1, C=C: E.tensor_single_scalar(C, v1, EPS, OP.add))
                    yf = newton_rsqrt(E, op, C, w, 7)
                    op(lambda yf=yf, P2=P2: E.tensor_single_scalar(coef[:, 16 + 2 * P2:18 + 2 * P2], yf, 1.0, OP.mult))
                    op(lambda mu=mu, yf=yf, Dv=Dv: E.tensor_tensor(Dv, mu, yf, OP.mult))
                    op(lambda Dv=Dv, P2=P2: E.tensor_single_scalar(coef[:, 20 + 2 * P2:22 + 2 * P2], Dv, -1.0, OP.mult), f"st2:{p}:{P2}")
                    if p == 0 and P2 == 1:
                        for k in range(9):
                            conv_op(3 * NZ + k)
                for h in range(NH):
                    sgu = p * NSLAB + 4 * NZ + 2 * h + 1
                    u.wait(E, f"su:{sgu}")
                    u.wait(E, f"sil:{p}:{h}")
                    op(lambda h=h: E.scalar_tensor_tensor(actb[:, h * T:(h + 1) * T],
                                                          pb[2 * (h % 2) + 1][:, :], suct[:, h:h + 1],
                                                          gsc[4 * (h % 2)][:, :], OP.mult, OP.mult), f"actf:{p}:{h}")
                    if p == 0 and h <= 40:
                        conv_op(3 * NZ + 9 + 3 * h + 2)
                if p + 1 < NP:
                    ln1_block(p + 1)
                for P2 in range(2):
                    last_slab = p * NSLAB + 4 * NZ + 2 * NH + (P2 + 1) * NH - 1
                    u.wait(E, f"su:{last_slab}")
                    for tci in range(2):
                        tc = 2 * P2 + tci
                        for j in range(4):
                            if tc > 0:
                                u.wait(E, f"oD:{p}:{tc - 1}:{j}")
                            elif p > 0:
                                u.wait(E, f"oD:{p - 1}:3:{j}")
                            op(lambda p=p, tc=tc, tci=tci, j=j: E.tensor_tensor(
                                gsc[2 * j + 1][:, :], xslot_j(p, tc, j),
                                pb[(4 if tci == 0 else 0) + j][:, :], OP.add),
                                f"fin:{p}:{tc}" if j == 3 else f"finj:{p}:{tc}:{j}")

        def gen_scalar(E):
            def op(fn, label=None):
                if E:
                    E.wait_ge(sems["s"], u.cnt["s"])
                i = fn() if E else None
                u.inc("s", 1, label, i)

            def sconv_op(i_):
                u.wait(E, f"qld:{i_}")
                kind = ITEMS[i_][0]
                if p0_direct(kind):
                    g_ = slab_of_item(i_)
                    out_ap = wb[g_ % NWB][:, :]
                    if g_ >= NWB:
                        u.wait(E, f"su:{g_ - NWB}")
                        pk, px = slab_list(0)[g_ - NWB][0], slab_list(0)[g_ - NWB][1]
                        if p0_direct(pk):
                            u.wait(E, f"wqst:{item_of_slab(pk, px)}")
                    op(lambda: E.activation(out_ap, ring_u8[i_ % 4], AF.Identity,
                                            bias=m128[:, 0:1], scale=1.0), f"wqts:{i_}")
                    u.alias(f"sl:{g_}", f"wqts:{i_}")
                else:
                    out_ap = qo[QOSLOT[i_]][:, :]
                    if QOPREV[i_] is not None:
                        u.wait(E, f"wqst:{QOPREV[i_]}")
                    op(lambda: E.activation(out_ap, ring_f32[i_ % 4], AF.Copy,
                                            bias=0.0, scale=1.0), f"wqts:{i_}")

            u.wait(E, "dtmin")
            op(lambda: E.activation(dt_exp, dt_row, AF.Exp, scale=-1.0, bias=ntmp[0:1, 0:1]), "dtexp")
            for p in range(NP):
                if p == 0:
                    u.wait(E, "st1:0")
                    for tc in range(4):
                        op(lambda tc=tc: E.activation(xn4[:, tc * D:(tc + 1) * D], xslot(0, tc), AF.Identity,
                                                      scale=coef[:, 0 + tc:1 + tc], bias=coef[:, 4 + tc:5 + tc]),
                           f"xn1:0:{tc}")
                for k in range(ND):
                    u.wait(E, f"T1mm:{p}:{k}")
                    if p == 0 and k == 0:
                        u.wait(E, "cst")
                    op(lambda k=k: E.activation(xnT[:, k * T:(k + 1) * T], pb[4 + k % 2][:, :], AF.Identity,
                                                scale=gcct[:, k:k + 1], bias=bcct[:, k:k + 1]), f"T1ev:{p}:{k}")
                for e in range(NZ):
                    if e == 0 and p > 0:
                        for jj in range(4):
                            u.wait(E, f"oD:{p - 1}:3:{jj}")
                    if e >= 2:
                        u.wait(E, f"gat:{p}:{e - 2}")
                    b = 4 * (e % 2)
                    u.wait(E, f"zc:{p}:{e}")
                    op(lambda b=b: E.activation(gsc[b + 1][:, :], gsc[b][:, :], AF.Sigmoid), f"sz:{p}:{e}")
                    u.wait(E, f"gc:{p}:{e}")
                    op(lambda b=b: E.activation(gsc[b + 3][:, :], gsc[b + 2][:, :], AF.Sigmoid), f"sg2:{p}:{e}")
                    if p == 0 and e == 0:
                        sconv_op(2)
                        sconv_op(5)
                    if p == 0 and e + 2 < NZ:
                        sconv_op(3 * (e + 2) + 2)
                seq = [(2 * P2s + tci, j) for P2s in range(2) for tci in range(2) for j in range(4)]
                for k, (tcs, j) in enumerate(seq):
                    if j == 0 and tcs % 2 == 0:
                        u.wait(E, f"su:{k2g_m2(p, tcs // 2)}")
                    if k >= 2:
                        ptc, pj = seq[k - 2]
                        u.wait(E, f"x2:{p}:{ptc}" if pj == 3 else f"x2j:{p}:{ptc}:{pj}")
                    tci = tcs % 2
                    op(lambda p=p, tcs=tcs, tci=tci, j=j: E.activation(
                        gsc[4 * (j % 2)][:, :], pb[(4 if tci == 0 else 0) + j][:, :], AF.Identity,
                        scale=1.0, bias=coef[:, 12 + tcs:13 + tcs]), f"ev:{p}:{tcs}:{j}")
                u.wait(E, f"st2:{p}:0")
                for tc in range(2):
                    op(lambda p=p, tc=tc: E.activation(xn4[:, tc * D:(tc + 1) * D], xslot(p, tc), AF.Identity,
                                                       scale=coef[:, 16 + tc:17 + tc], bias=coef[:, 20 + tc:21 + tc]),
                       f"xnf:{p}:{tc}")
                for k in range(ND):
                    u.wait(E, f"T2Amm:{p}:{k}")
                    op(lambda k=k: E.activation(xnT[:, k * T:k * T + 256], pb[4 + k % 2][:, 0:256], AF.Identity),
                       f"T2evA:{p}:{k}")
                u.wait(E, f"st2:{p}:1")
                for tc in range(2, 4):
                    op(lambda p=p, tc=tc: E.activation(xn4[:, tc * D:(tc + 1) * D], xslot(p, tc), AF.Identity,
                                                       scale=coef[:, 16 + tc:17 + tc], bias=coef[:, 20 + tc:21 + tc]),
                       f"xnf:{p}:{tc}")
                for k in range(ND):
                    u.wait(E, f"T2Bmm:{p}:{k}")
                    op(lambda k=k: E.activation(xnT[:, k * T + 256:(k + 1) * T], pb[4 + k % 2][:, 256:512], AF.Identity),
                       f"T2evB:{p}:{k}")
                for h in range(NH):
                    if h == 0:
                        u.wait(E, f"gat:{p}:{NZ - 1}")
                    if h >= 2:
                        u.wait(E, f"actf:{p}:{h - 2}")
                    sgg = p * NSLAB + 4 * NZ + 2 * h
                    u.wait(E, f"su:{sgg}")
                    op(lambda h=h: E.activation(gsc[4 * (h % 2)][:, :], pb[2 * (h % 2)][:, :], AF.Silu,
                                                scale=sgct[:, h:h + 1]), f"sil:{p}:{h}")
                    if p == 0 and h <= 40:
                        sconv_op(3 * NZ + 9 + 3 * h)
                        sconv_op(3 * NZ + 9 + 3 * h + 1)
                if p + 1 < NP:
                    u.wait(E, f"st1:{p + 1}")
                    for tc in range(4):
                        op(lambda p=p, tc=tc: E.activation(xn4[:, tc * D:(tc + 1) * D], xslot(p + 1, tc), AF.Identity,
                                                           scale=coef[:, 0 + tc:1 + tc], bias=coef[:, 4 + tc:5 + tc]),
                           f"xn1:{p + 1}:{tc}")

        gens = dict(gpsimd=gen_gpsimd, sync=gen_sync, tensor=gen_tensor,
                    vector=gen_vector2, scalar=gen_scalar)
        for name, g in gens.items():
            g(None)
        u.start_emit(sems)

        with nc.Block() as blk:
            @blk.gpsimd
            def _(e):
                for s in sems.values():
                    e.sem_clear(s)
                e.memset(m128[:, :], -128.0)

        with nc.Block() as blk:
            @blk.sync
            def _(e):
                gens["sync"](e)

            @blk.tensor
            def _(e):
                gens["tensor"](e)

            @blk.vector
            def _(e):
                gens["vector"](e)

            @blk.scalar
            def _(e):
                gens["scalar"](e)

            @blk.gpsimd
            def _(e):
                gens["gpsimd"](e)

    return nc


_BF = ml_dtypes.bfloat16


def _host_prep(inputs):
    x = np.ascontiguousarray(np.asarray(inputs["L0_attn_x"], dtype=np.float32).reshape(NCORES * TPC, D))
    Win = np.asarray(inputs["L0_attn_W_in"], dtype=np.float32)
    Wout = np.asarray(inputs["L0_attn_W_out"], dtype=np.float32)
    gamma = np.asarray(inputs["L0_attn_gamma"], dtype=np.float32)
    beta = np.asarray(inputs["L0_attn_beta"], dtype=np.float32)
    dt = np.asarray(inputs["L0_attn_dt"], dtype=np.float32)
    wq_g = np.asarray(inputs["L0_ffn_wq_gate"], dtype=np.int32)
    wq_u = np.asarray(inputs["L0_ffn_wq_up"], dtype=np.int32)
    wq_d = np.asarray(inputs["L0_ffn_wq_down"], dtype=np.int32)
    s_g = np.asarray(inputs["L0_ffn_s_gate"], dtype=np.float32)
    s_u = np.asarray(inputs["L0_ffn_s_up"], dtype=np.float32)
    s_d = np.asarray(inputs["L0_ffn_s_down"], dtype=np.float32)

    def rowslab(WT, nchunk):
        return np.ascontiguousarray(WT.reshape(ND, 128, nchunk, 128).transpose(2, 1, 0, 3).reshape(nchunk, 128, D))

    winT_h = np.ascontiguousarray(rowslab(Win.T, NE2)[ZIG])
    woutT_h = np.ascontiguousarray(Wout.T)
    def lowbyte(a):
        return np.ascontiguousarray(a.view(np.uint8)[..., 0::4])

    wqg_h = lowbyte(rowslab(wq_g.T, NH))
    wqu_h = lowbyte(rowslab(wq_u.T, NH))
    wqd_h = lowbyte(np.ascontiguousarray(wq_d.T))
    sgc_h = np.ascontiguousarray(s_g.reshape(NH, 128).T)
    suc_h = np.ascontiguousarray(s_u.reshape(NH, 128).T)
    gcc_h = np.ascontiguousarray(gamma.reshape(ND, 128).T)
    bcc_h = np.ascontiguousarray(beta.reshape(ND, 128).T)
    sdb_h = np.ascontiguousarray(np.broadcast_to(s_d, (128, D)))
    dtr_h = np.ascontiguousarray(dt.reshape(1, 4096))
    idm_h = np.eye(128, dtype=_BF)

    shared = dict(winT=winT_h, woutT=woutT_h, wqg=wqg_h, wqu=wqu_h, wqd=wqd_h,
                  sgc=sgc_h, suc=suc_h, gcc=gcc_h, bcc=bcc_h, sdb=sdb_h,
                  dtr=dtr_h, idm=idm_h)
    in_maps = []
    for c in range(NCORES):
        m = dict(shared)
        m["x"] = np.ascontiguousarray(x[c * TPC:(c + 1) * TPC])
        in_maps.append(m)
    return in_maps


def kernel(**inputs):
    in_maps = _host_prep(inputs)
    nc = build_program()
    res = run_bass_kernel_spmd(nc, in_maps, core_ids=list(range(NCORES)))
    out = np.concatenate([np.asarray(r["out"], dtype=np.float32) for r in res.results], axis=0)
    return out.reshape(4, 8192, D)
